# version 10
# baseline (speedup 1.0000x reference)
import numpy as np
from contextlib import ExitStack

import concourse.bass as bass
import concourse.mybir as mybir
from concourse.alu_op_type import AluOpType as OP
from concourse.bass_utils import run_bass_kernel_spmd

F32 = mybir.dt.float32
BF16 = mybir.dt.bfloat16
FP8 = mybir.dt.float8e4
I32 = mybir.dt.int32
AF = mybir.ActivationFunctionType
AX = mybir.AxisListType.X

D = 2048
S = 4096
TOWN = 2048
H = 16
FI = 6144
NCH = 16
NO = 48
OQ = 12
TB = 512
SCALE = float(1.0 / np.sqrt(128.0))
EPS = 1e-5

C_ONES = 0
C_BQ = 1
C_BK = C_BQ + NCH
C_BO = C_BK + NCH
C_GS = C_BO + NCH
C_US = C_GS + NO
C_DSH = C_US + NO
C_TOT = C_DSH + NCH


class P:

    def __init__(self):
        self.sem = {}
        self.free = {}

    def add(self, name, handle):
        self.sem[name] = [handle, 0]

    def cnt(self, name):
        return self.sem[name][1]

    def inc(self, ins, name, k=1):
        h = self.sem[name]
        ins.then_inc(h[0], k)
        h[1] += k
        return h[1]

    def wait(self, eng, name, val):
        h = self.sem[name]
        assert val <= h[1], f"forward wait {name}: {val} > {h[1]}"
        if val > 0:
            eng.wait_ge(h[0], val)

    def guard(self, eng, res):
        v = self.free.get(res)
        if v is not None:
            self.wait(eng, v[0], v[1])

    def rel(self, res, semname):
        self.free[res] = (semname, self.cnt(semname))


def build(stop=None, p1_tbs=8, ffn_mode='full', olim=48, skip_sv=0,
          p2_tcs=8, f_tcs=4, f_hs=16, races=True):

    nc = bass.Bass("TRN2", target_bir_lowering=False, detect_race_conditions=races)

    ap = {}
    ap["x0t"] = nc.dram_tensor("x0t", [D, S], F32, kind="ExternalInput").ap()
    for nm, shp in [
        ("wq_codes", [D, D]), ("wk_codes", [D, D]), ("wv_codes", [D, D]),
        ("wo_codes", [D, D]),
        ("wq_sx", [D, D]), ("wk_sx", [D, D]), ("wv_sx", [D, D]), ("wo_sx", [D, D]),
        ("bq2", [128, NCH]), ("bk2", [128, NCH]), ("bo2", [128, NCH]),
        ("bv_row", [1, D]),
        ("gate_w", [D, FI]), ("up_w", [D, FI]), ("down_w", [FI, D]),
        ("gs2", [128, NO]), ("us2", [128, NO]), ("ds2", [128, NCH]),
    ]:
        ap[nm] = nc.dram_tensor(nm, shp, F32, kind="ExternalInput").ap()
    out_t = nc.dram_tensor("out_t", [D, TOWN], F32, kind="ExternalOutput").ap()

    x1t = nc.dram_tensor("x1t", [D, S], F32, kind="Internal").ap()
    g8 = nc.dram_tensor("g8", [NO, NCH, 128, 128], FP8, kind="Internal").ap()
    u8 = nc.dram_tensor("u8", [NO, NCH, 128, 128], FP8, kind="Internal").ap()
    d8 = nc.dram_tensor("d8", [NCH, NO, 128, 128], FP8, kind="Internal").ap()
    wqb = nc.dram_tensor("wqb", [H, NCH, 128, 128], BF16, kind="Internal").ap()
    wkb = nc.dram_tensor("wkb", [H, NCH, 128, 128], BF16, kind="Internal").ap()
    wvb = nc.dram_tensor("wvb", [D, D], BF16, kind="Internal").ap()
    wob = nc.dram_tensor("wob", [NCH, H, 128, 128], BF16, kind="Internal").ap()
    qt_hi = nc.dram_tensor("qt_hi", [H, 128, TOWN], BF16, kind="Internal").ap()
    qt_lo = nc.dram_tensor("qt_lo", [H, 128, TOWN], BF16, kind="Internal").ap()
    kt_hi = nc.dram_tensor("kt_hi", [H, 128, S], BF16, kind="Internal").ap()
    kt_lo = nc.dram_tensor("kt_lo", [H, 128, S], BF16, kind="Internal").ap()
    v_hi = nc.dram_tensor("v_hi", [H, 128, S // 128, 128], BF16, kind="Internal").ap()
    v_lo = nc.dram_tensor("v_lo", [H, 128, S // 128, 128], BF16, kind="Internal").ap()

    with ExitStack() as st:
        E = st.enter_context
        X2 = E(nc.sbuf_tensor("X2", [128, NCH * TB], F32))
        SCR = E(nc.sbuf_tensor("SCR", [128, 6 * 512], F32))
        BVB = E(nc.sbuf_tensor("BVB", [128, D], F32))
        MRB = E(nc.sbuf_tensor("MRB", [128, 1024], F32))
        CST = E(nc.sbuf_tensor("CST", [128, C_TOT], F32))
        MCOL = E(nc.sbuf_tensor("MCOL", [128, H], F32))
        ONESB = E(nc.sbuf_tensor("ONESB", [128, 1], BF16))
        ROW1 = E(nc.sbuf_tensor("ROW1", [1, 128], F32))
        ROWS = E(nc.sbuf_tensor("ROWS", [1, 2048], F32))
        RSUM = E(nc.sbuf_tensor("RSUM", [1, 512], F32))
        QN = E(nc.sbuf_tensor("QN", [1, H], F32))
        KN = E(nc.sbuf_tensor("KN", [1, H], F32))
        M16 = E(nc.sbuf_tensor("M16", [1, H], F32))
        TMP1 = E(nc.sbuf_tensor("TMP1", [1, 16], F32))
        A1 = E(nc.sbuf_tensor("A1", [128, 8192], BF16))
        A2 = E(nc.sbuf_tensor("A2", [128, 8192], BF16))
        A3 = E(nc.sbuf_tensor("A3", [128, 6144], BF16))
        A4 = E(nc.sbuf_tensor("A4", [128, 6144], BF16))
        A5 = E(nc.sbuf_tensor("A5", [128, 8192], BF16))
        A6 = E(nc.sbuf_tensor("A6", [128, 6144], BF16))
        WI = E(nc.sbuf_tensor("WI", [128, 4096], I32))
        W8A = E(nc.sbuf_tensor("W8A", [128, 4096], FP8))
        W8B = E(nc.sbuf_tensor("W8B", [128, 8192], FP8))

        pA = E(nc.psum_tensor("pA", [128, 512], F32))
        pB = E(nc.psum_tensor("pB", [128, 512], F32))
        pC = E(nc.psum_tensor("pC", [128, 512], F32))
        pD = E(nc.psum_tensor("pD", [128, 512], F32))
        pE = E(nc.psum_tensor("pE", [128, 512], F32))
        pM = E(nc.psum_tensor("pM", [128, 512], F32))
        pR = E(nc.psum_tensor("pR", [1, 512], F32))
        pN = E(nc.psum_tensor("pN", [1, 512], F32))

        s_ld = E(nc.semaphore("s_ld"))
        s_g = E(nc.semaphore("s_g"))
        s_sc = E(nc.semaphore("s_sc"))
        s_t = E(nc.semaphore("s_t"))
        s_v = E(nc.semaphore("s_v"))
        s_s = E(nc.semaphore("s_s"))
        block = E(nc.Block())

        p = P()
        for nm, h in [("ld", s_ld), ("g", s_g), ("sc", s_sc),
                      ("t", s_t), ("v", s_v), ("s", s_s)]:
            p.add(nm, h)

        def body(engname, fn):
            getattr(block, engname)(fn)

        def _(e):
            for o_, i_ in [
                (CST[:, C_BQ:C_BQ + NCH], ap["bq2"]),
                (CST[:, C_BK:C_BK + NCH], ap["bk2"]),
                (CST[:, C_BO:C_BO + NCH], ap["bo2"]),
                (CST[:, C_GS:C_GS + NO], ap["gs2"]),
                (CST[:, C_US:C_US + NO], ap["us2"]),
                (CST[:, C_DSH:C_DSH + NCH], ap["ds2"]),
                (ROWS[0:1, 0:D], ap["bv_row"]),
            ]:
                p.inc(e.dma_start(out=o_, in_=i_), "ld", 16)
        body("sync", _)
        ld_consts = p.cnt("ld")

        def _(e):
            e.memset(CST[:, C_ONES:C_ONES + 1], 1.0)
            e.memset(ONESB[:, 0:1], 1.0)
            e.memset(ROW1[0:1, 0:128], 1.0)
            p.wait(e, "ld", ld_consts)
            ins = e.tensor_scalar(CST[:, C_DSH:C_DSH + NCH],
                                  CST[:, C_DSH:C_DSH + NCH], 0.5, None, OP.mult)
            p.inc(ins, "v")
        body("vector", _)
        v_consts = p.cnt("v")

        for fb in range(4):
            bank = pM if fb % 2 == 0 else pE
            bname = "pM" if fb % 2 == 0 else "pE"

            def _(e, fb=fb, bank=bank, bname=bname):
                if fb == 0:
                    p.wait(e, "v", v_consts)
                p.guard(e, bname)
                ins = e.matmul(bank[:, 0:512], ROW1[0:1, 0:128],
                               ROWS[0:1, fb * 512:(fb + 1) * 512],
                               start=True, stop=True)
                p.inc(ins, "t")
            body("tensor", _)
            tv = p.cnt("t")

            def _(e, fb=fb, bank=bank, tv=tv):
                p.wait(e, "t", tv)
                ins = e.tensor_copy(BVB[:, fb * 512:(fb + 1) * 512], bank[:, 0:512])
                p.inc(ins, "v")
            body("vector", _)
            p.rel(bname, "v")

        def prep_q(src_ap_2048, dst_store_fn, qi):
            par = qi % 4
            xq = X2[:, par * 2048:(par + 1) * 2048]

            def _(e, src_ap_2048=src_ap_2048, par=par, xq=xq):
                p.guard(e, f"X2q{par}")
                ins = e.dma_start(out=xq, in_=src_ap_2048)
                p.inc(ins, "ld", 16)
            body("sync", _)
            ldv = p.cnt("ld")
            eng = "vector" if qi % 2 == 0 else "scalar"
            sem = "v" if qi % 2 == 0 else "s"
            half = qi % 2
            rot = (qi // 2) % 2
            f8a = W8A[:, half * 2048:(half + 1) * 2048]
            wi = WI[:, half * 2048:(half + 1) * 2048]
            f8b = W8B[:, (half * 2 + rot) * 2048:(half * 2 + rot + 1) * 2048]
            wkey = f"W8{eng[0]}{rot}"

            def _(e, ldv=ldv, xq=xq, f8a=f8a, wi=wi, f8b=f8b, eng=eng, sem=sem,
                  wkey=wkey):
                p.wait(e, "ld", ldv)
                p.guard(e, wkey)
                if eng == "vector":
                    e.tensor_copy(f8a, xq)
                    e.tensor_scalar(wi, f8a, 512.0, None, OP.mult)
                    ins = e.tensor_scalar(f8b, wi, 1.0 / 512.0, None, OP.mult)
                else:
                    e.copy(f8a, xq)
                    e.mul(wi, f8a, 512.0)
                    ins = e.mul(f8b, wi, 1.0 / 512.0)
                p.inc(ins, sem)
            body(eng, _)
            cv = p.cnt(sem)
            p.free[f"X2q{par}"] = (sem, cv)

            def _(e, cv=cv, sem=sem, f8b=f8b):
                p.wait(e, sem, cv)
                ins = dst_store_fn(e, f8b)
                p.inc(ins, "g", 16)
            body("gpsimd", _)
            p.free[wkey] = ("g", p.cnt("g"))

        qi = 0
        for srcw, dstw in ((ap["gate_w"], g8), (ap["up_w"], u8)):
            for c in range(NCH):
                for sub in range(3):
                    o0 = sub * 16

                    def store(e, f8b, dstw=dstw, c=c, o0=o0):
                        return e.dma_start(
                            out=dstw[o0:o0 + 16, c].rearrange("o p w -> p o w"),
                            in_=f8b.rearrange("p (o w) -> p o w", w=128))
                    prep_q(srcw[c * 128:(c + 1) * 128,
                                sub * 2048:(sub + 1) * 2048], store, qi)
                    qi += 1
        for o in range(NO):
            def store(e, f8b, o=o):
                return e.dma_start(
                    out=d8[:, o].rearrange("d p w -> p d w"),
                    in_=f8b.rearrange("p (d w) -> p d w", w=128))
            prep_q(ap["down_w"][o * 128:(o + 1) * 128, :], store, qi)
            qi += 1
        g_after_ffn8 = p.cnt("g")

        def prep_attn(codes, sx, store_fn):
            for c in range(NCH):
                par = c % 2
                xc = X2[:, par * 4096:par * 4096 + 2048]
                xs = X2[:, par * 4096 + 2048:par * 4096 + 4096]
                dq = A1[:, par * 2048:(par + 1) * 2048]

                def _(e, c=c, xc=xc, xs=xs, par=par):
                    p.guard(e, f"X2h{par}")
                    p.guard(e, f"X2q{2 * par}")
                    p.guard(e, f"X2q{2 * par + 1}")
                    p.inc(e.dma_start(out=xc,
                                      in_=codes[c * 128:(c + 1) * 128, :]), "ld", 16)
                    p.inc(e.dma_start(out=xs,
                                      in_=sx[c * 128:(c + 1) * 128, :]), "ld", 16)
                body("sync", _)
                ldv = p.cnt("ld")

                def _(e, c=c, ldv=ldv, xc=xc, xs=xs, dq=dq, par=par):
                    p.wait(e, "ld", ldv)
                    p.guard(e, f"A1h{par}")
                    ins = e.tensor_tensor(dq, xc, xs, OP.mult)
                    p.inc(ins, "v")
                body("vector", _)
                vv = p.cnt("v")
                p.free[f"X2h{par}"] = ("v", vv)

                def _(e, c=c, vv=vv, dq=dq):
                    p.wait(e, "v", vv)
                    ins = store_fn(e, c, dq)
                    p.inc(ins, "sc", 16)
                body("scalar", _)
                p.free[f"A1h{par}"] = ("sc", p.cnt("sc"))

        prep_attn(ap["wq_codes"], ap["wq_sx"],
                  lambda e, c, dq: e.dma_start(
                      out=wqb[:, c].rearrange("f p w -> p f w"),
                      in_=dq.rearrange("p (f w) -> p f w", w=128)))
        prep_attn(ap["wk_codes"], ap["wk_sx"],
                  lambda e, c, dq: e.dma_start(
                      out=wkb[:, c].rearrange("f p w -> p f w"),
                      in_=dq.rearrange("p (f w) -> p f w", w=128)))
        prep_attn(ap["wv_codes"], ap["wv_sx"],
                  lambda e, c, dq: e.dma_start(
                      out=wvb[c * 128:(c + 1) * 128, :], in_=dq))
        prep_attn(ap["wo_codes"], ap["wo_sx"],
                  lambda e, c, dq: e.dma_start(
                      out=wob[:, c].rearrange("d p w -> p d w"),
                      in_=dq.rearrange("p (d w) -> p d w", w=128)))
        p.rel("A1", "sc")
        p.rel("X2", "v")
        g_after_p0 = p.cnt("g")
        sc_after_p0 = p.cnt("sc")

        def dummy_out(x_sem):
            for tcx in range(4):
                def _(e, tcx=tcx, x_sem=x_sem):
                    p.wait(e, x_sem[0], x_sem[1])
                    ins = e.dma_start(
                        out=out_t[:, tcx * TB:(tcx + 1) * TB]
                        .rearrange("(c p) t -> p c t", p=128),
                        in_=X2[:, :].rearrange("p (c t) -> p c t", t=TB))
                    p.inc(ins, "g", 16)
                body("gpsimd", _)
            gf = p.cnt("g")

            def _(e, gf=gf):
                p.wait(e, "g", gf)
            body("gpsimd", _)

        if stop == 'p0':
            dummy_out(("g", g_after_p0))
            return nc

        def emit_ln(x_ready):
            for c in range(NCH):
                slot = SCR[:, (c % 2) * 512:(c % 2) * 512 + 512]

                def _(e, c=c, slot=slot):
                    if c == 0:
                        p.wait(e, x_ready[0], x_ready[1])
                    p.guard(e, f"SCR{c % 2}")
                    ins = e.activation(slot, X2[:, c * TB:(c + 1) * TB], AF.Square)
                    p.inc(ins, "s")
                body("scalar", _)
                sv = p.cnt("s")

                def _(e, c=c, slot=slot, sv=sv, x_ready=x_ready):
                    if c == 0:
                        p.wait(e, x_ready[0], x_ready[1])
                        p.guard(e, "pR")
                        p.guard(e, "pN")
                    p.wait(e, "s", sv)
                    e.matmul(pR[0:1, :], CST[:, C_ONES:C_ONES + 1],
                             X2[:, c * TB:(c + 1) * TB],
                             start=(c == 0), stop=(c == NCH - 1))
                    ins = e.matmul(pN[0:1, :], CST[:, C_ONES:C_ONES + 1], slot,
                                   start=(c == 0), stop=(c == NCH - 1))
                    p.inc(ins, "t")
                body("tensor", _)
                p.free[f"SCR{c % 2}"] = ("t", p.cnt("t"))
            t_stats = p.cnt("t")

            def _(e):
                p.wait(e, "t", t_stats)
                e.tensor_scalar(ROWS[0:1, 0:512], pR[0:1, :], 1.0 / D, None, OP.mult)
                e.tensor_scalar(ROWS[0:1, 512:1024], pN[0:1, :], 1.0 / D, None, OP.mult)
                e.tensor_tensor(ROWS[0:1, 1536:2048], ROWS[0:1, 0:512],
                                ROWS[0:1, 0:512], OP.mult)
                e.tensor_tensor(ROWS[0:1, 1024:1536], ROWS[0:1, 512:1024],
                                ROWS[0:1, 1536:2048], OP.subtract)
                ins = e.tensor_scalar(ROWS[0:1, 1024:1536], ROWS[0:1, 1024:1536],
                                      EPS, None, OP.add)
                p.inc(ins, "v")
            body("vector", _)
            p.rel("pR", "v")
            p.rel("pN", "v")
            vv = p.cnt("v")

            def _(e, vv=vv):
                p.wait(e, "v", vv)
                ins = e.activation(ROWS[0:1, 512:1024], ROWS[0:1, 1024:1536],
                                   AF.Sqrt)
                p.inc(ins, "s")
            body("scalar", _)
            sv = p.cnt("s")

            def _(e, sv=sv):
                p.wait(e, "s", sv)
                e.reciprocal(ROWS[0:1, 1536:2048], ROWS[0:1, 512:1024])
                e.tensor_tensor(RSUM[0:1, :], ROWS[0:1, 1536:2048],
                                ROWS[0:1, 1536:2048], OP.mult)
                e.tensor_tensor(RSUM[0:1, :], RSUM[0:1, :],
                                ROWS[0:1, 1024:1536], OP.mult)
                e.tensor_scalar(RSUM[0:1, :], RSUM[0:1, :], -0.5, 1.5,
                                OP.mult, OP.add)
                ins = e.tensor_tensor(ROWS[0:1, 512:1024], ROWS[0:1, 1536:2048],
                                      RSUM[0:1, :], OP.mult)
                p.inc(ins, "v")
            body("vector", _)
            vv = p.cnt("v")

            def _(e, vv=vv):
                p.wait(e, "v", vv)
                p.guard(e, "pM")
                p.guard(e, "pE")
                e.matmul(pM[:, :], ROW1[0:1, :], ROWS[0:1, 0:512],
                         start=True, stop=True)
                ins = e.matmul(pE[:, :], ROW1[0:1, :], ROWS[0:1, 512:1024],
                               start=True, stop=True)
                p.inc(ins, "t")
            body("tensor", _)
            tv = p.cnt("t")

            def _(e, tv=tv):
                p.wait(e, "t", tv)
                e.tensor_copy(MRB[:, 0:512], pM[:, :])
                ins = e.tensor_copy(MRB[:, 512:1024], pE[:, :])
                p.inc(ins, "v")
                p.guard(e, "A1")
                p.guard(e, "A2")
                for c in range(NCH):
                    e.tensor_tensor(SCR[:, 0:512], X2[:, c * TB:(c + 1) * TB],
                                    MRB[:, 0:512], OP.subtract)
                    e.tensor_tensor(SCR[:, 512:1024], SCR[:, 0:512],
                                    MRB[:, 512:1024], OP.mult)
                    e.tensor_copy(A1[:, c * TB:(c + 1) * TB], SCR[:, 512:1024])
                    ins = e.tensor_tensor(A2[:, c * TB:(c + 1) * TB],
                                          SCR[:, 512:1024],
                                          A1[:, c * TB:(c + 1) * TB], OP.subtract)
                p.inc(ins, "v")
            body("vector", _)
            p.rel("pM", "v")
            p.rel("pE", "v")
            return p.cnt("v")

        def emit_ffn(x_ready, w_ready_g):
            v_yt = emit_ln(x_ready)
            if ffn_mode == 'ln':
                return v_yt
            gbanks = [(pA, "pA"), (pC, "pC")]
            ubanks = [(pB, "pB"), (pD, "pD")]
            for q in range(4):
                for oi in range(OQ):
                    o = q * OQ + oi
                    if o >= olim:
                        continue
                    par = o % 2
                    gslot = A5[:, par * 4096:par * 4096 + 2048]
                    uslot = A5[:, par * 4096 + 2048:par * 4096 + 4096]

                    def _(e, o=o, par=par, gslot=gslot, uslot=uslot,
                          w_ready_g=w_ready_g):
                        if o == 0:
                            p.wait(e, "g", w_ready_g)
                        for k_ in (("A5q0", "A5q1") if par == 0 else ("A5q2", "A5q3")):
                            p.guard(e, k_)
                        p.inc(e.dma_start(
                            out=gslot.rearrange("p (c w) -> p c w", w=128),
                            in_=g8[o].rearrange("c p w -> p c w")), "g", 16)
                        p.inc(e.dma_start(
                            out=uslot.rearrange("p (c w) -> p c w", w=128),
                            in_=u8[o].rearrange("c p w -> p c w")), "g", 16)
                    body("gpsimd", _)
                    gv = p.cnt("g")

                    def _(e, o=o, par=par, gslot=gslot, uslot=uslot, gv=gv,
                          v_yt=v_yt):
                        p.wait(e, "g", gv)
                        if o == 0:
                            p.wait(e, "v", v_yt)
                        gp, gn = gbanks[par]
                        up, un = ubanks[par]
                        p.guard(e, gn)
                        p.guard(e, un)
                        for c in range(NCH):
                            lt = gslot[:, c * 128:(c + 1) * 128]
                            e.matmul(gp[:, :], lt, A1[:, c * TB:(c + 1) * TB],
                                     start=(c == 0), stop=False)
                            ins = e.matmul(gp[:, :], lt, A2[:, c * TB:(c + 1) * TB],
                                           start=False, stop=(c == NCH - 1))
                        p.inc(ins, "t")
                        for c in range(NCH):
                            lt = uslot[:, c * 128:(c + 1) * 128]
                            e.matmul(up[:, :], lt, A1[:, c * TB:(c + 1) * TB],
                                     start=(c == 0), stop=False)
                            ins = e.matmul(up[:, :], lt, A2[:, c * TB:(c + 1) * TB],
                                           start=False, stop=(c == NCH - 1))
                        p.inc(ins, "t")
                    body("tensor", _)
                    for k_ in (("A5q0", "A5q1") if par == 0 else ("A5q2", "A5q3")):
                        p.free[k_] = ("t", p.cnt("t"))
                    tv = p.cnt("t")
                    if skip_sv >= 2:
                        p.free[gbanks[par][1]] = ("t", p.cnt("t"))
                        p.free[ubanks[par][1]] = ("t", p.cnt("t"))
                        continue

                    def _(e, o=o, par=par, tv=tv):
                        p.wait(e, "t", tv)
                        p.guard(e, f"SCR{par}")
                        gp, _gn = gbanks[par]
                        sil = SCR[:, par * 512:par * 512 + 512]
                        ins = e.activation(sil, gp[:, :], AF.Silu,
                                           scale=CST[:, C_GS + o:C_GS + o + 1])
                        p.inc(ins, "s")
                    body("scalar", _)
                    p.free[gbanks[par][1]] = ("s", p.cnt("s"))
                    sv = p.cnt("s")
                    if skip_sv >= 1:
                        continue

                    def _(e, o=o, oi=oi, par=par, sv=sv):
                        p.wait(e, "s", sv)
                        p.guard(e, f"SCR{2 + par}")
                        if oi == 0:
                            p.guard(e, "SCR4")
                            for k_ in ("A3", "A4w0", "A4w1", "A4r"):
                                p.guard(e, k_)
                        up, _un = ubanks[par]
                        sil = SCR[:, par * 512:par * 512 + 512]
                        upz = SCR[:, 1024 + par * 512:1024 + par * 512 + 512]
                        ff = SCR[:, 2048:2560]
                        e.tensor_scalar(upz, up[:, :],
                                        CST[:, C_US + o:C_US + o + 1], None,
                                        OP.mult)
                        e.tensor_tensor(ff, sil, upz, OP.mult)
                        e.tensor_copy(A3[:, oi * 512:(oi + 1) * 512], ff)
                        ins = e.tensor_tensor(A4[:, oi * 512:(oi + 1) * 512],
                                              ff, A3[:, oi * 512:(oi + 1) * 512],
                                              OP.subtract)
                        p.inc(ins, "v")
                    body("vector", _)
                    p.free[ubanks[par][1]] = ("v", p.cnt("v"))
                    p.free[f"SCR{par}"] = ("v", p.cnt("v"))
                    p.free[f"SCR{2 + par}"] = ("v", p.cnt("v"))
                v_fuse = p.cnt("v")
                if ffn_mode == 'gu':
                    for k_ in ("A3", "A4w0", "A4w1", "A4r"):
                        p.free[k_] = ("v", p.cnt("v"))
                    continue

                dbanks = [(pA, "pA"), (pB, "pB")]
                for dg in range(8):
                    par = dg % 2
                    dwslot = A6[:, par * 3072:par * 3072 + 3072]

                    def _(e, q=q, dg=dg, par=par, dwslot=dwslot):
                        for j_ in range(6):
                            p.guard(e, f"A6_{par * 6 + j_}")
                        for dd in range(2):
                            p.inc(e.dma_start(
                                out=dwslot[:, dd * 1536:(dd + 1) * 1536]
                                .rearrange("p (o w) -> p o w", w=128),
                                in_=d8[dg * 2 + dd, q * OQ:(q + 1) * OQ]
                                .rearrange("o p w -> p o w")), "g", 16)
                    body("gpsimd", _)
                    gv = p.cnt("g")

                    for dd in range(2):
                        d = dg * 2 + dd
                        bank, bname = dbanks[d % 2]

                        def _(e, q=q, d=d, dd=dd, par=par, dwslot=dwslot,
                              gv=gv, bank=bank, bname=bname, v_fuse=v_fuse):
                            if dd == 0:
                                p.wait(e, "g", gv)
                                if d == 0:
                                    p.wait(e, "v", v_fuse)
                            p.guard(e, bname)
                            for oi in range(OQ):
                                lt = dwslot[:, dd * 1536 + oi * 128:
                                            dd * 1536 + (oi + 1) * 128]
                                e.matmul(bank[:, :], lt,
                                         A3[:, oi * 512:(oi + 1) * 512],
                                         start=(oi == 0), stop=False)
                                ins = e.matmul(bank[:, :], lt,
                                               A4[:, oi * 512:(oi + 1) * 512],
                                               start=False, stop=(oi == OQ - 1))
                            p.inc(ins, "t")
                        body("tensor", _)
                        tv = p.cnt("t")

                        def _(e, d=d, bank=bank, tv=tv):
                            p.wait(e, "t", tv)
                            ins = e.scalar_tensor_tensor(
                                X2[:, d * TB:(d + 1) * TB], bank[:, :],
                                CST[:, C_DSH + d:C_DSH + d + 1],
                                X2[:, d * TB:(d + 1) * TB], OP.mult, OP.add)
                            p.inc(ins, "v")
                        body("vector", _)
                        p.free[bname] = ("v", p.cnt("v"))
                    for j_ in range(6):
                        p.free[f"A6_{par * 6 + j_}"] = ("t", p.cnt("t"))
                for k_ in ("A3", "A4w0", "A4w1", "A4r"):
                    p.free[k_] = ("t", p.cnt("t"))
            p.rel("A1", "v")
            p.rel("A2", "v")
            return p.cnt("v")

        x1_store_g = []
        for tb in range(p1_tbs):
            def _(e, tb=tb):
                p.guard(e, "X2")
                ins = e.dma_start(
                    out=X2[:, :].rearrange("p (c t) -> p c t", t=TB),
                    in_=ap["x0t"][:, tb * TB:(tb + 1) * TB]
                    .rearrange("(c p) t -> p c t", p=128))
                p.inc(ins, "ld", 16)
            body("sync", _)
            xv = p.cnt("ld")
            v_done = emit_ffn(("ld", xv), g_after_ffn8)

            def _(e, tb=tb, v_done=v_done):
                p.wait(e, "v", v_done)
                ins = e.dma_start(
                    out=x1t[:, tb * TB:(tb + 1) * TB]
                    .rearrange("(c p) t -> p c t", p=128),
                    in_=X2[:, :].rearrange("p (c t) -> p c t", t=TB))
                p.inc(ins, "g", 16)
            body("gpsimd", _)
            x1_store_g.append(p.cnt("g"))
            p.rel("X2", "g")

        if stop == 'p1':
            dummy_out(("g", p.cnt("g")))
            return nc

        kq_banks = [(pA, "pA"), (pB, "pB")]
        for tc in range(p2_tcs):
            def _(e, tc=tc):
                p.guard(e, "X2")
                p.wait(e, "g", x1_store_g[min(tc, len(x1_store_g) - 1)])
                ins = e.dma_start(
                    out=X2[:, :].rearrange("p (c t) -> p c t", t=TB),
                    in_=x1t[:, tc * TB:(tc + 1) * TB]
                    .rearrange("(c p) t -> p c t", p=128))
                p.inc(ins, "ld", 16)
            body("sync", _)
            xv = p.cnt("ld")
            v_yt = emit_ln(("ld", xv))

            projs = [("k", wkb, C_BK, kt_hi, kt_lo, KN, tc * TB)]
            if tc < 4:
                projs.append(("q", wqb, C_BQ, qt_hi, qt_lo, QN, tc * TB))
            for pname, wtile, cbias, dhi, dlo, nmax, coff in projs:
                for f in range(H):
                    par = f % 2
                    wpar = f % 4
                    wslot = A5[:, wpar * 2048:wpar * 2048 + 2048]

                    def _(e, f=f, wpar=wpar, wslot=wslot, wtile=wtile):
                        p.guard(e, f"A5q{wpar}")
                        p.wait(e, "sc", sc_after_p0)
                        ins = e.dma_start(
                            out=wslot.rearrange("p (c w) -> p c w", w=128),
                            in_=wtile[f].rearrange("c p w -> p c w"))
                        p.inc(ins, "ld", 16)
                    body("sync", _)
                    ldv = p.cnt("ld")
                    bank, bname = kq_banks[par]

                    def _(e, f=f, par=par, wslot=wslot, ldv=ldv, bank=bank,
                          bname=bname, v_yt=v_yt):
                        p.wait(e, "ld", ldv)
                        if f == 0:
                            p.wait(e, "v", v_yt)
                        p.guard(e, bname)
                        for c in range(NCH):
                            lt = wslot[:, c * 128:(c + 1) * 128]
                            e.matmul(bank[:, :], lt, A1[:, c * TB:(c + 1) * TB],
                                     start=(c == 0), stop=False)
                            ins = e.matmul(bank[:, :], lt, A2[:, c * TB:(c + 1) * TB],
                                           start=False, stop=(c == NCH - 1))
                        p.inc(ins, "t")
                    body("tensor", _)
                    p.free[f"A5q{wpar}"] = ("t", p.cnt("t"))
                    tv = p.cnt("t")

                    kslot = 1024 + par * 1024

                    def _(e, f=f, par=par, bank=bank, tv=tv, cbias=cbias,
                          kslot=kslot):
                        p.wait(e, "t", tv)
                        p.guard(e, f"SCR{par}")
                        kf = SCR[:, 0:512] if par == 0 else SCR[:, 512:1024]
                        ins = e.activation(kf, bank[:, :], AF.Identity,
                                           bias=CST[:, cbias + f:cbias + f + 1])
                        p.inc(ins, "s")
                    body("scalar", _)
                    p.free[bname] = ("s", p.cnt("s"))
                    sv = p.cnt("s")

                    def _(e, f=f, par=par, sv=sv, kslot=kslot):
                        p.wait(e, "s", sv)
                        p.guard(e, f"A6_{par}")
                        kf = SCR[:, 0:512] if par == 0 else SCR[:, 512:1024]
                        ins = e.tensor_tensor(A6[:, par * 512:par * 512 + 512],
                                              kf, kf, OP.mult)
                        p.inc(ins, "v")
                        p.guard(e, f"A6_{2 + 2 * par}")
                        p.guard(e, f"A6_{3 + 2 * par}")
                        e.tensor_copy(A6[:, kslot:kslot + 512], kf)
                        ins = e.tensor_tensor(A6[:, kslot + 512:kslot + 1024],
                                              kf, A6[:, kslot:kslot + 512],
                                              OP.subtract)
                        p.inc(ins, "v")
                    body("vector", _)
                    p.free[f"SCR{par}"] = ("v", p.cnt("v"))
                    vv = p.cnt("v")

                    nb, nbk = (pN, "pN") if f % 2 == 0 else (pR, "pR")

                    def _(e, f=f, par=par, vv=vv, nb=nb, nbk=nbk):
                        p.wait(e, "v", vv - 1)
                        p.guard(e, nbk)
                        ins = e.matmul(nb[0:1, :], ONESB[:, 0:1],
                                       A6[:, par * 512:par * 512 + 512],
                                       start=True, stop=True)
                        p.inc(ins, "t")
                    body("tensor", _)
                    p.free[f"A6_{par}"] = ("t", p.cnt("t"))
                    tnv = p.cnt("t")

                    def _(e, f=f, tc=tc, tnv=tnv, nmax=nmax, nb=nb):
                        p.wait(e, "t", tnv)
                        if tc == 0:
                            ins = e.tensor_reduce(nmax[0:1, f:f + 1], nb[0:1, :],
                                                  AX, OP.max)
                        else:
                            e.tensor_reduce(TMP1[0:1, 0:1], nb[0:1, :], AX, OP.max)
                            ins = e.tensor_tensor(nmax[0:1, f:f + 1],
                                                  nmax[0:1, f:f + 1],
                                                  TMP1[0:1, 0:1], OP.max)
                        p.inc(ins, "v")
                    body("vector", _)
                    p.free[nbk] = ("v", p.cnt("v"))

                    def _(e, f=f, par=par, vv=vv, dhi=dhi, dlo=dlo, coff=coff,
                          kslot=kslot):
                        p.wait(e, "v", vv)
                        p.inc(e.dma_start(out=dhi[f][:, coff:coff + 512],
                                          in_=A6[:, kslot:kslot + 512]), "g", 16)
                        p.inc(e.dma_start(out=dlo[f][:, coff:coff + 512],
                                          in_=A6[:, kslot + 512:kslot + 1024]),
                              "g", 16)
                    body("gpsimd", _)
                    p.free[f"A6_{2 + 2 * par}"] = ("g", p.cnt("g"))
                    p.free[f"A6_{3 + 2 * par}"] = ("g", p.cnt("g"))

            v_banks = [(pC, "pC"), (pD, "pD")]
            for fb in range(4):
                for half in range(2):
                    def _(e, tc=tc, fb=fb, half=half):
                        p.guard(e, f"A5q{half * 2}")
                        p.guard(e, f"A5q{half * 2 + 1}")
                        p.wait(e, "sc", sc_after_p0)
                        ins = e.dma_start(
                            out=A5[:, half * 4096:(half + 1) * 4096]
                            .rearrange("p (c w) -> p c w", w=512),
                            in_=wvb[half * 1024:(half + 1) * 1024,
                                    fb * 512:(fb + 1) * 512]
                            .rearrange("(c p) w -> p c w", p=128))
                        p.inc(ins, "ld", 16)
                    body("sync", _)
                ldv = p.cnt("ld")
                for tt in range(4):
                    bank, bname = v_banks[tt % 2]

                    def _(e, fb=fb, tt=tt, ldv=ldv, bank=bank, bname=bname,
                          v_yt=v_yt):
                        p.wait(e, "ld", ldv)
                        if fb == 0 and tt == 0:
                            p.wait(e, "v", v_yt)
                        p.guard(e, bname)
                        for c in range(NCH):
                            half = c // 8
                            wsl = A5[:, half * 4096 + (c % 8) * 512:
                                     half * 4096 + (c % 8) * 512 + 512]
                            lt1 = A1[:, c * TB + tt * 128:c * TB + (tt + 1) * 128]
                            lt2 = A2[:, c * TB + tt * 128:c * TB + (tt + 1) * 128]
                            e.matmul(bank[:, :], lt1, wsl,
                                     start=(c == 0), stop=False)
                            ins = e.matmul(bank[:, :], lt2, wsl,
                                           start=False, stop=(c == NCH - 1))
                        p.inc(ins, "t")
                    body("tensor", _)
                    if tt == 3:
                        for k_ in ("A5q0", "A5q1", "A5q2", "A5q3"):
                            p.free[k_] = ("t", p.cnt("t"))
                    tv = p.cnt("t")
                    vslot = 2048 + (tt % 2) * 2048

                    def _(e, fb=fb, tt=tt, tv=tv, bank=bank, vslot=vslot):
                        p.wait(e, "t", tv)
                        p.guard(e, f"A6_{4 + 4 * (tt % 2)}")
                        p.guard(e, f"A6_{5 + 4 * (tt % 2)}")
                        vf = SCR[:, 2048:2560]
                        e.tensor_tensor(vf, bank[:, :],
                                        BVB[:, fb * 512:(fb + 1) * 512], OP.add)
                        e.tensor_copy(A6[:, vslot:vslot + 512], vf)
                        ins = e.tensor_tensor(A6[:, vslot + 512:vslot + 1024],
                                              vf, A6[:, vslot:vslot + 512],
                                              OP.subtract)
                        p.inc(ins, "v")
                    body("vector", _)
                    p.free[bname] = ("v", p.cnt("v"))
                    vv = p.cnt("v")

                    def _(e, tc=tc, fb=fb, tt=tt, vv=vv, vslot=vslot):
                        p.wait(e, "v", vv)
                        tile = tc * 4 + tt
                        for j in range(4):
                            h = fb * 4 + j
                            p.inc(e.dma_start(
                                out=v_hi[h][:, tile, :],
                                in_=A6[:, vslot + j * 128:vslot + (j + 1) * 128]),
                                "g", 16)
                    body("gpsimd", _)
                    p.free[f"A6_{4 + 4 * (tt % 2)}"] = ("g", p.cnt("g"))

                    def _(e, tc=tc, fb=fb, tt=tt, vv=vv, vslot=vslot):
                        p.wait(e, "v", vv)
                        tile = tc * 4 + tt
                        for j in range(4):
                            h = fb * 4 + j
                            p.inc(e.dma_start(
                                out=v_lo[h][:, tile, :],
                                in_=A6[:, vslot + 512 + j * 128:
                                       vslot + 512 + (j + 1) * 128]), "sc", 16)
                    body("scalar", _)
                    p.free[f"A6_{5 + 4 * (tt % 2)}"] = ("sc", p.cnt("sc"))
            p.rel("X2", "v")
            p.rel("A1", "t")
            p.rel("A2", "t")
        g_after_p2 = p.cnt("g")
        sc_after_p2 = p.cnt("sc")
        ld_after_p2 = p.cnt("ld")

        if stop == 'p2':
            dummy_out(("sc", p.cnt("sc")))
            return nc

        def _(e):
            p.wait(e, "v", p.cnt("v"))
            ins = e.tensor_tensor(TMP1[0:1, 0:H], QN[0:1, :], KN[0:1, :], OP.mult)
            p.inc(ins, "v")
        body("vector", _)
        vv = p.cnt("v")

        def _(e, vv=vv):
            p.wait(e, "v", vv)
            ins = e.activation(M16[0:1, :], TMP1[0:1, 0:H], AF.Sqrt)
            p.inc(ins, "s")
        body("scalar", _)
        sv = p.cnt("s")

        def _(e, sv=sv):
            p.wait(e, "s", sv)
            ins = e.tensor_scalar(M16[0:1, :], M16[0:1, :],
                                  -1.01 * SCALE, None, OP.mult)
            p.inc(ins, "v")
        body("vector", _)
        vv = p.cnt("v")
        for hh in range(H):
            def _(e, hh=hh, vv=vv):
                if hh == 0:
                    p.wait(e, "v", vv)
                p.guard(e, "pM")
                ins = e.matmul(pM[:, 0:1], ROW1[0:1, :], M16[0:1, hh:hh + 1],
                               start=True, stop=True)
                p.inc(ins, "t")
            body("tensor", _)
            tv = p.cnt("t")

            def _(e, hh=hh, tv=tv):
                p.wait(e, "t", tv)
                ins = e.tensor_copy(MCOL[:, hh:hh + 1], pM[:, 0:1])
                p.inc(ins, "v")
            body("vector", _)
            p.rel("pM", "v")

        st_banks = [(pA, "pA"), (pB, "pB")]
        for tc in range(f_tcs):
            def _(e, tc=tc):
                p.guard(e, "X2")
                p.wait(e, "g", x1_store_g[min(tc, len(x1_store_g) - 1)])
                ins = e.dma_start(
                    out=X2[:, :].rearrange("p (c t) -> p c t", t=TB),
                    in_=x1t[:, tc * TB:(tc + 1) * TB]
                    .rearrange("(c p) t -> p c t", p=128))
                p.inc(ins, "ld", 16)
            body("sync", _)
            xv = p.cnt("ld")

            def _(e, tc=tc, xv=xv):
                p.wait(e, "ld", xv)
                ins = None
                for c in range(NCH):
                    ins = e.activation(X2[:, c * TB:(c + 1) * TB],
                                       X2[:, c * TB:(c + 1) * TB], AF.Identity,
                                       bias=CST[:, C_BO + c:C_BO + c + 1])
                p.inc(ins, "s")
            body("scalar", _)
            s_x2 = p.cnt("s")
            p.free["X2p"] = ("s", s_x2)

            for h in range(f_hs):
                g_ = h // 4
                def _(e, tc=tc, h=h):
                    p.guard(e, "A1")
                    p.guard(e, "A5q0")
                    p.wait(e, "g", g_after_p2)
                    p.inc(e.dma_start(out=A1[:, 0:4096], in_=kt_hi[h]), "ld", 16)
                    p.inc(e.dma_start(out=A1[:, 4096:8192], in_=kt_lo[h]), "ld", 16)
                    p.inc(e.dma_start(out=A5[:, 0:512],
                                      in_=qt_hi[h][:, tc * 512:(tc + 1) * 512]),
                          "ld", 16)
                    p.inc(e.dma_start(out=A5[:, 512:1024],
                                      in_=qt_lo[h][:, tc * 512:(tc + 1) * 512]),
                          "ld", 16)
                body("sync", _)
                ldv = p.cnt("ld")

                def _(e, tc=tc, h=h):
                    p.guard(e, "A2")
                    p.wait(e, "sc", sc_after_p2)
                    p.wait(e, "g", g_after_p2)
                    p.inc(e.dma_start(
                        out=A2[:, 0:4096],
                        in_=v_hi[h].rearrange("p T w -> p (T w)")), "sc", 16)
                    p.inc(e.dma_start(
                        out=A2[:, 4096:8192],
                        in_=v_lo[h].rearrange("p T w -> p (T w)")), "sc", 16)
                body("scalar", _)
                scv = p.cnt("sc")

                st4 = [(pA, "pA"), (pB, "pB"), (pD, "pD"), (pE, "pE")]
                RACC2 = SCR[:, 2048:3072]
                OF = MRB[:, 512:1024]

                def emit_st(e, sq, first_ld=None):
                    bank, bname = st4[sq % 4]
                    if first_ld is not None:
                        p.wait(e, "ld", first_ld)
                    p.guard(e, bname)
                    kh = A1[:, sq * 128:(sq + 1) * 128]
                    kl = A1[:, 4096 + sq * 128:4096 + (sq + 1) * 128]
                    e.matmul(bank[:, :], kh, A5[:, 0:512],
                             start=True, stop=False)
                    e.matmul(bank[:, :], kh, A5[:, 512:1024],
                             start=False, stop=False)
                    ins = e.matmul(bank[:, :], kl, A5[:, 0:512],
                                   start=False, stop=True)
                    p.inc(ins, "t")

                def emit_pv(e, sq):
                    vh = A2[:, sq * 128:(sq + 1) * 128]
                    vl = A2[:, 4096 + sq * 128:4096 + (sq + 1) * 128]
                    par = sq % 4
                    ph = A6[:, par * 512:par * 512 + 512]
                    pl = A6[:, 2048 + par * 512:2048 + par * 512 + 512]
                    e.matmul(pC[:, :], vh, ph, start=(sq == 0), stop=False)
                    e.matmul(pC[:, :], vh, pl, start=False, stop=False)
                    ins = e.matmul(pC[:, :], vl, ph, start=False,
                                   stop=(sq == 31))
                    p.inc(ins, "t")

                def _(e, h=h, ldv=ldv, scv=scv):
                    p.guard(e, "pC")
                    p.wait(e, "sc", scv)
                    for sq in range(4):
                        emit_st(e, sq, first_ld=(ldv if sq == 0 else None))
                body("tensor", _)
                st_tv = [p.cnt("t") - 3 + i for i in range(4)]
                pair_v = []

                for pr in range(16):
                    s0, s1 = 2 * pr, 2 * pr + 1
                    pa0, pa1 = s0 % 4, s1 % 4

                    def _(e, h=h, s0=s0, s1=s1, pa0=pa0, pa1=pa1,
                          tv=st_tv[s1]):
                        p.wait(e, "t", tv)
                        for sq, par in ((s0, pa0), (s1, pa1)):
                            p.guard(e, f"SCR{par}")
                            p.guard(e, f"A6_{par}")
                            pf = SCR[:, par * 512:par * 512 + 512]
                            bank, _b = st4[par]
                            e.activation(pf, bank[:, :], AF.Exp,
                                         bias=MCOL[:, h:h + 1], scale=SCALE)
                            ins = e.copy(A6[:, par * 512:par * 512 + 512], pf)
                        p.inc(ins, "s")
                    body("scalar", _)
                    p.free[st4[pa0][1]] = ("s", p.cnt("s"))
                    p.free[st4[pa1][1]] = ("s", p.cnt("s"))
                    sv = p.cnt("s")

                    def _(e, pr=pr, s0=s0, pa0=pa0, sv=sv):
                        p.wait(e, "s", sv)
                        p.guard(e, f"A6_{4 + pa0}")
                        p.guard(e, f"A6_{5 + pa0}")
                        pfp = SCR[:, pa0 * 512:pa0 * 512 + 1024]
                        if pr == 0:
                            p.guard(e, "SCR4")
                            e.tensor_copy(RACC2, pfp)
                        else:
                            e.tensor_tensor(RACC2, RACC2, pfp, OP.add)
                        ins = e.tensor_tensor(
                            A6[:, 2048 + pa0 * 512:2048 + pa0 * 512 + 1024],
                            pfp, A6[:, pa0 * 512:pa0 * 512 + 1024], OP.subtract)
                        p.inc(ins, "v")
                    body("vector", _)
                    p.free[f"SCR{pa0}"] = ("v", p.cnt("v"))
                    p.free[f"SCR{pa1}"] = ("v", p.cnt("v"))
                    pair_v.append(p.cnt("v"))

                    if pr < 14:
                        def _(e, s0=s0, s1=s1):
                            emit_st(e, s0 + 4)
                            emit_st(e, s1 + 4)
                        body("tensor", _)
                        st_tv.extend([p.cnt("t") - 1, p.cnt("t")])
                    if pr >= 1:
                        def _(e, pr=pr):
                            p.wait(e, "v", pair_v[pr - 1])
                            emit_pv(e, 2 * (pr - 1))
                            emit_pv(e, 2 * (pr - 1) + 1)
                        body("tensor", _)
                        pp = (2 * (pr - 1)) % 4
                        p.free[f"A6_{pp}"] = ("t", p.cnt("t"))
                        p.free[f"A6_{pp + 1}"] = ("t", p.cnt("t"))
                        p.free[f"A6_{4 + pp}"] = ("t", p.cnt("t"))
                        p.free[f"A6_{5 + pp}"] = ("t", p.cnt("t"))

                def _(e):
                    p.wait(e, "v", pair_v[15])
                    emit_pv(e, 30)
                    emit_pv(e, 31)
                body("tensor", _)
                p.free["A6_2"] = ("t", p.cnt("t"))
                p.free["A6_3"] = ("t", p.cnt("t"))
                p.free["A6_6"] = ("t", p.cnt("t"))
                p.free["A6_7"] = ("t", p.cnt("t"))
                p.free["A5q0"] = ("t", p.cnt("t"))
                p.free["A1"] = ("t", p.cnt("t"))
                p.free["A2"] = ("t", p.cnt("t"))

                def _(e):
                    p.guard(e, "pR")
                    e.matmul(pR[0:1, :], CST[:, C_ONES:C_ONES + 1],
                             SCR[:, 2048:2560], start=True, stop=False)
                    ins = e.matmul(pR[0:1, :], CST[:, C_ONES:C_ONES + 1],
                                   SCR[:, 2560:3072], start=False, stop=True)
                    p.inc(ins, "t")
                body("tensor", _)
                p.free["SCR4"] = ("t", p.cnt("t"))
                tv = p.cnt("t")

                def _(e, tv=tv):
                    p.wait(e, "t", tv)
                    ins = e.reciprocal(RSUM[0:1, :], pR[0:1, :])
                    p.inc(ins, "v")
                body("vector", _)
                p.rel("pR", "v")
                vv = p.cnt("v")

                def _(e, vv=vv):
                    p.wait(e, "v", vv)
                    p.guard(e, "pM")
                    ins = e.matmul(pM[:, :], ROW1[0:1, :], RSUM[0:1, :],
                                   start=True, stop=True)
                    p.inc(ins, "t")
                body("tensor", _)
                tv = p.cnt("t")

                def _(e, h=h, tv=tv):
                    p.wait(e, "t", tv)
                    p.guard(e, "A3")
                    e.tensor_copy(MRB[:, 0:512], pM[:, :])
                    of = OF
                    e.tensor_tensor(of, pC[:, :], MRB[:, 0:512], OP.mult)
                    hh = h % 4
                    e.tensor_copy(A3[:, hh * 512:(hh + 1) * 512], of)
                    ins = e.tensor_tensor(A3[:, 2048 + hh * 512:
                                             2048 + (hh + 1) * 512], of,
                                          A3[:, hh * 512:(hh + 1) * 512],
                                          OP.subtract)
                    p.inc(ins, "v")
                body("vector", _)
                p.rel("pM", "v")
                p.rel("pC", "v")
                p.free["SCR3"] = ("v", p.cnt("v"))
                v_og = p.cnt("v")

                if h % 4 == 3:
                    pj_banks = [(pD, "pD"), (pE, "pE")]
                    for c in range(NCH):
                        par = c % 2

                        def _(e, tc=tc, h=h, c=c, par=par):
                            p.guard(e, f"A4w{par}")
                            p.wait(e, "sc", sc_after_p0)
                            ins = e.dma_start(
                                out=A4[:, par * 512:par * 512 + 512]
                                .rearrange("p (j w) -> p j w", w=128),
                                in_=wob[c, (h - 3):(h + 1)]
                                .rearrange("j p w -> p j w"))
                            p.inc(ins, "ld", 16)
                        body("sync", _)
                        ldv = p.cnt("ld")
                        bank, bname = pj_banks[par]

                        def _(e, c=c, par=par, ldv=ldv, bank=bank, bname=bname,
                              v_og=v_og):
                            p.wait(e, "ld", ldv)
                            if c == 0:
                                p.wait(e, "v", v_og)
                            p.guard(e, bname)
                            for j in range(4):
                                wt = A4[:, par * 512 + j * 128:
                                        par * 512 + (j + 1) * 128]
                                e.matmul(bank[:, :], wt,
                                         A3[:, j * 512:(j + 1) * 512],
                                         start=(j == 0), stop=False)
                                ins = e.matmul(bank[:, :], wt,
                                               A3[:, 2048 + j * 512:
                                                  2048 + (j + 1) * 512],
                                               start=False, stop=(j == 3))
                            p.inc(ins, "t")
                        body("tensor", _)
                        p.free[f"A4w{par}"] = ("t", p.cnt("t"))
                        tv = p.cnt("t")

                        def _(e, c=c, tv=tv, bank=bank, s_x2=s_x2):
                            p.wait(e, "t", tv)
                            if c == 0:
                                p.wait(e, "s", s_x2)
                            ins = e.tensor_tensor(X2[:, c * TB:(c + 1) * TB],
                                                  X2[:, c * TB:(c + 1) * TB],
                                                  bank[:, :], OP.add)
                            p.inc(ins, "v")
                        body("vector", _)
                        p.free[bname] = ("v", p.cnt("v"))
                    p.free["A3"] = ("t", p.cnt("t"))

            if stop == 'flash':
                v_done = p.cnt("v")
            else:
                v_done = emit_ffn(("v", p.cnt("v")), g_after_p0)

            def _(e, tc=tc, v_done=v_done):
                p.wait(e, "v", v_done)
                ins = e.dma_start(
                    out=out_t[:, tc * TB:(tc + 1) * TB]
                    .rearrange("(c p) t -> p c t", p=128),
                    in_=X2[:, :].rearrange("p (c t) -> p c t", t=TB))
                p.inc(ins, "g", 16)
            body("gpsimd", _)
            p.rel("X2", "g")

        g_final = p.cnt("g")

        def _(e, g_final=g_final):
            p.wait(e, "g", g_final)
        body("gpsimd", _)

    return nc


_NC_CACHE = {}


def kernel(L0_attn_x, L0_attn_wq_codes, L0_attn_wq_scales, L0_attn_bq,
           L0_attn_wk_codes, L0_attn_wk_scales, L0_attn_bk,
           L0_attn_wv_codes, L0_attn_wv_scales, L0_attn_bv,
           L0_attn_wo_codes, L0_attn_wo_scales, L0_attn_bo,
           L0_ffn_gate_w, L0_ffn_up_w, L0_ffn_down_w,
           L0_ffn_gate_scale, L0_ffn_up_scale, L0_ffn_down_scale):
    x = np.asarray(L0_attn_x, dtype=np.float32)

    def col2(b):
        return np.ascontiguousarray(np.asarray(b, np.float32).reshape(NCH, 128).T)

    def col2w(b):
        return np.ascontiguousarray(np.asarray(b, np.float32).reshape(NO, 128).T)

    shared = {
        "wq_codes": np.ascontiguousarray(L0_attn_wq_codes, np.float32),
        "wk_codes": np.ascontiguousarray(L0_attn_wk_codes, np.float32),
        "wv_codes": np.ascontiguousarray(L0_attn_wv_codes, np.float32),
        "wo_codes": np.ascontiguousarray(L0_attn_wo_codes, np.float32),
        "wq_sx": np.ascontiguousarray(np.repeat(np.asarray(L0_attn_wq_scales, np.float32), 32, axis=0)),
        "wk_sx": np.ascontiguousarray(np.repeat(np.asarray(L0_attn_wk_scales, np.float32), 32, axis=0)),
        "wv_sx": np.ascontiguousarray(np.repeat(np.asarray(L0_attn_wv_scales, np.float32), 32, axis=0)),
        "wo_sx": np.ascontiguousarray(np.repeat(np.asarray(L0_attn_wo_scales, np.float32), 32, axis=0)),
        "bq2": col2(L0_attn_bq), "bk2": col2(L0_attn_bk), "bo2": col2(L0_attn_bo),
        "bv_row": np.ascontiguousarray(np.asarray(L0_attn_bv, np.float32).reshape(1, D)),
        "gate_w": np.ascontiguousarray(L0_ffn_gate_w, np.float32),
        "up_w": np.ascontiguousarray(L0_ffn_up_w, np.float32),
        "down_w": np.ascontiguousarray(L0_ffn_down_w, np.float32),
        "gs2": col2w(L0_ffn_gate_scale), "us2": col2w(L0_ffn_up_scale),
        "ds2": col2(L0_ffn_down_scale),
    }

    in_maps = []
    for core in range(8):
        b, half = core // 2, core % 2
        own = x[b, half * TOWN:(half + 1) * TOWN, :]
        other = x[b, (1 - half) * TOWN:(2 - half) * TOWN, :]
        x0t = np.ascontiguousarray(np.concatenate([own, other], axis=0).T)
        m = dict(shared)
        m["x0t"] = x0t
        in_maps.append(m)

    if "nc" not in _NC_CACHE:
        _NC_CACHE["nc"] = build()
    nc = _NC_CACHE["nc"]
    res = run_bass_kernel_spmd(nc, in_maps, core_ids=list(range(8)))

    out = np.empty((4, S, D), dtype=np.float32)
    for core in range(8):
        b, half = core // 2, core % 2
        out[b, half * TOWN:(half + 1) * TOWN, :] = res.results[core]["out_t"].T
    return out


if __name__ == "__main__":
    import time
    t0 = time.time()
    nc = build()
    print(f"build {time.time() - t0:.1f}s")


# version 11
# speedup vs baseline: 1.0483x; 1.0483x over previous
import numpy as np
from contextlib import ExitStack

import concourse.bass as bass
import concourse.mybir as mybir
from concourse.alu_op_type import AluOpType as OP
from concourse.bass_utils import run_bass_kernel_spmd

F32 = mybir.dt.float32
BF16 = mybir.dt.bfloat16
FP8 = mybir.dt.float8e4
I32 = mybir.dt.int32
AF = mybir.ActivationFunctionType
AX = mybir.AxisListType.X

D = 2048
S = 4096
TOWN = 2048
H = 16
FI = 6144
NCH = 16
NO = 48
OQ = 12
TB = 512
SCALE = float(1.0 / np.sqrt(128.0))
EPS = 1e-5

C_ONES = 0
C_BQ = 1
C_BK = C_BQ + NCH
C_BO = C_BK + NCH
C_GS = C_BO + NCH
C_US = C_GS + NO
C_DSH = C_US + NO
C_TOT = C_DSH + NCH


class P:

    def __init__(self):
        self.sem = {}
        self.free = {}

    def add(self, name, handle):
        self.sem[name] = [handle, 0]

    def cnt(self, name):
        return self.sem[name][1]

    def inc(self, ins, name, k=1):
        h = self.sem[name]
        ins.then_inc(h[0], k)
        h[1] += k
        return h[1]

    def wait(self, eng, name, val):
        h = self.sem[name]
        assert val <= h[1], f"forward wait {name}: {val} > {h[1]}"
        if val > 0:
            eng.wait_ge(h[0], val)

    def guard(self, eng, res):
        v = self.free.get(res)
        if v is not None:
            self.wait(eng, v[0], v[1])

    def rel(self, res, semname):
        self.free[res] = (semname, self.cnt(semname))


def build(stop=None, p1_tbs=8, ffn_mode='full', olim=48, skip_sv=0,
          p2_tcs=8, f_tcs=4, f_hs=16, races=True):

    nc = bass.Bass("TRN2", target_bir_lowering=False, detect_race_conditions=races)

    ap = {}
    ap["x0t"] = nc.dram_tensor("x0t", [D, S], F32, kind="ExternalInput").ap()
    for nm, shp in [
        ("wq_codes", [D, D]), ("wk_codes", [D, D]), ("wv_codes", [D, D]),
        ("wo_codes", [D, D]),
        ("wq_sx", [D, D]), ("wk_sx", [D, D]), ("wv_sx", [D, D]), ("wo_sx", [D, D]),
        ("bq2", [128, NCH]), ("bk2", [128, NCH]), ("bo2", [128, NCH]),
        ("bv_row", [1, D]),
        ("gate_w", [D, FI]), ("up_w", [D, FI]), ("down_w", [FI, D]),
        ("gs2", [128, NO]), ("us2", [128, NO]), ("ds2", [128, NCH]),
    ]:
        ap[nm] = nc.dram_tensor(nm, shp, F32, kind="ExternalInput").ap()
    out_t = nc.dram_tensor("out_t", [D, TOWN], F32, kind="ExternalOutput").ap()

    x1t = nc.dram_tensor("x1t", [D, S], F32, kind="Internal").ap()
    g8 = nc.dram_tensor("g8", [NO, NCH, 128, 128], FP8, kind="Internal").ap()
    u8 = nc.dram_tensor("u8", [NO, NCH, 128, 128], FP8, kind="Internal").ap()
    d8 = nc.dram_tensor("d8", [NCH, NO, 128, 128], FP8, kind="Internal").ap()
    wqb = nc.dram_tensor("wqb", [H, NCH, 128, 128], BF16, kind="Internal").ap()
    wkb = nc.dram_tensor("wkb", [H, NCH, 128, 128], BF16, kind="Internal").ap()
    wvb = nc.dram_tensor("wvb", [D, D], BF16, kind="Internal").ap()
    wob = nc.dram_tensor("wob", [NCH, H, 128, 128], BF16, kind="Internal").ap()
    qt_hi = nc.dram_tensor("qt_hi", [H, 128, TOWN], BF16, kind="Internal").ap()
    qt_lo = nc.dram_tensor("qt_lo", [H, 128, TOWN], BF16, kind="Internal").ap()
    kt_hi = nc.dram_tensor("kt_hi", [H, 128, S], BF16, kind="Internal").ap()
    kt_lo = nc.dram_tensor("kt_lo", [H, 128, S], BF16, kind="Internal").ap()
    v_hi = nc.dram_tensor("v_hi", [H, 128, S // 128, 128], BF16, kind="Internal").ap()
    v_lo = nc.dram_tensor("v_lo", [H, 128, S // 128, 128], BF16, kind="Internal").ap()

    with ExitStack() as st:
        E = st.enter_context
        X2 = E(nc.sbuf_tensor("X2", [128, NCH * TB], F32))
        SCR = E(nc.sbuf_tensor("SCR", [128, 6 * 512], F32))
        BVB = E(nc.sbuf_tensor("BVB", [128, D], F32))
        MRB = E(nc.sbuf_tensor("MRB", [128, 1024], F32))
        CST = E(nc.sbuf_tensor("CST", [128, C_TOT], F32))
        MCOL = E(nc.sbuf_tensor("MCOL", [128, H], F32))
        ONESB = E(nc.sbuf_tensor("ONESB", [128, 1], BF16))
        ROW1 = E(nc.sbuf_tensor("ROW1", [1, 128], F32))
        ROWS = E(nc.sbuf_tensor("ROWS", [1, 2048], F32))
        RSUM = E(nc.sbuf_tensor("RSUM", [1, 512], F32))
        QN = E(nc.sbuf_tensor("QN", [1, H], F32))
        KN = E(nc.sbuf_tensor("KN", [1, H], F32))
        M16 = E(nc.sbuf_tensor("M16", [1, H], F32))
        TMP1 = E(nc.sbuf_tensor("TMP1", [1, 16], F32))
        A1 = E(nc.sbuf_tensor("A1", [128, 8192], BF16))
        A2 = E(nc.sbuf_tensor("A2", [128, 8192], BF16))
        A3 = E(nc.sbuf_tensor("A3", [128, 6144], BF16))
        A4 = E(nc.sbuf_tensor("A4", [128, 6144], BF16))
        A5 = E(nc.sbuf_tensor("A5", [128, 8192], BF16))
        A6 = E(nc.sbuf_tensor("A6", [128, 6144], BF16))
        WI = E(nc.sbuf_tensor("WI", [128, 4096], I32))
        W8A = E(nc.sbuf_tensor("W8A", [128, 4096], FP8))
        W8B = E(nc.sbuf_tensor("W8B", [128, 8192], FP8))

        pA = E(nc.psum_tensor("pA", [128, 512], F32))
        pB = E(nc.psum_tensor("pB", [128, 512], F32))
        pC = E(nc.psum_tensor("pC", [128, 512], F32))
        pD = E(nc.psum_tensor("pD", [128, 512], F32))
        pE = E(nc.psum_tensor("pE", [128, 512], F32))
        pM = E(nc.psum_tensor("pM", [128, 512], F32))
        pR = E(nc.psum_tensor("pR", [1, 512], F32))
        pN = E(nc.psum_tensor("pN", [1, 512], F32))

        s_ld = E(nc.semaphore("s_ld"))
        s_g = E(nc.semaphore("s_g"))
        s_sc = E(nc.semaphore("s_sc"))
        s_t = E(nc.semaphore("s_t"))
        s_v = E(nc.semaphore("s_v"))
        s_s = E(nc.semaphore("s_s"))
        block = E(nc.Block())

        p = P()
        for nm, h in [("ld", s_ld), ("g", s_g), ("sc", s_sc),
                      ("t", s_t), ("v", s_v), ("s", s_s)]:
            p.add(nm, h)

        def body(engname, fn):
            getattr(block, engname)(fn)

        def _(e):
            for o_, i_ in [
                (CST[:, C_BQ:C_BQ + NCH], ap["bq2"]),
                (CST[:, C_BK:C_BK + NCH], ap["bk2"]),
                (CST[:, C_BO:C_BO + NCH], ap["bo2"]),
                (CST[:, C_GS:C_GS + NO], ap["gs2"]),
                (CST[:, C_US:C_US + NO], ap["us2"]),
                (CST[:, C_DSH:C_DSH + NCH], ap["ds2"]),
                (ROWS[0:1, 0:D], ap["bv_row"]),
            ]:
                p.inc(e.dma_start(out=o_, in_=i_), "ld", 16)
        body("sync", _)
        ld_consts = p.cnt("ld")

        def _(e):
            e.memset(CST[:, C_ONES:C_ONES + 1], 1.0)
            e.memset(ONESB[:, 0:1], 1.0)
            e.memset(ROW1[0:1, 0:128], 1.0)
            p.wait(e, "ld", ld_consts)
            ins = e.tensor_scalar(CST[:, C_DSH:C_DSH + NCH],
                                  CST[:, C_DSH:C_DSH + NCH], 0.5, None, OP.mult)
            p.inc(ins, "v")
        body("vector", _)
        v_consts = p.cnt("v")

        for fb in range(4):
            bank = pM if fb % 2 == 0 else pE
            bname = "pM" if fb % 2 == 0 else "pE"

            def _(e, fb=fb, bank=bank, bname=bname):
                if fb == 0:
                    p.wait(e, "v", v_consts)
                p.guard(e, bname)
                ins = e.matmul(bank[:, 0:512], ROW1[0:1, 0:128],
                               ROWS[0:1, fb * 512:(fb + 1) * 512],
                               start=True, stop=True)
                p.inc(ins, "t")
            body("tensor", _)
            tv = p.cnt("t")

            def _(e, fb=fb, bank=bank, tv=tv):
                p.wait(e, "t", tv)
                ins = e.tensor_copy(BVB[:, fb * 512:(fb + 1) * 512], bank[:, 0:512])
                p.inc(ins, "v")
            body("vector", _)
            p.rel(bname, "v")

        def prep_q(src_ap_2048, dst_store_fn, qi):
            par = qi % 4
            xq = X2[:, par * 2048:(par + 1) * 2048]

            def _(e, src_ap_2048=src_ap_2048, par=par, xq=xq):
                p.guard(e, f"X2q{par}")
                ins = e.dma_start(out=xq, in_=src_ap_2048)
                p.inc(ins, "ld", 16)
            body("sync", _)
            ldv = p.cnt("ld")
            eng = "vector" if qi % 2 == 0 else "scalar"
            sem = "v" if qi % 2 == 0 else "s"
            half = qi % 2
            rot = (qi // 2) % 2
            f8a = W8A[:, half * 2048:(half + 1) * 2048]
            wi = WI[:, half * 2048:(half + 1) * 2048]
            f8b = W8B[:, (half * 2 + rot) * 2048:(half * 2 + rot + 1) * 2048]
            wkey = f"W8{eng[0]}{rot}"

            def _(e, ldv=ldv, xq=xq, f8a=f8a, wi=wi, f8b=f8b, eng=eng, sem=sem,
                  wkey=wkey):
                p.wait(e, "ld", ldv)
                p.guard(e, wkey)
                if eng == "vector":
                    e.tensor_copy(f8a, xq)
                    e.tensor_scalar(wi, f8a, 512.0, None, OP.mult)
                    ins = e.tensor_scalar(f8b, wi, 1.0 / 512.0, None, OP.mult)
                else:
                    e.copy(f8a, xq)
                    e.mul(wi, f8a, 512.0)
                    ins = e.mul(f8b, wi, 1.0 / 512.0)
                p.inc(ins, sem)
            body(eng, _)
            cv = p.cnt(sem)
            p.free[f"X2q{par}"] = (sem, cv)

            def _(e, cv=cv, sem=sem, f8b=f8b):
                p.wait(e, sem, cv)
                ins = dst_store_fn(e, f8b)
                p.inc(ins, "g", 16)
            body("gpsimd", _)
            p.free[wkey] = ("g", p.cnt("g"))

        qi = 0
        for srcw, dstw in ((ap["gate_w"], g8), (ap["up_w"], u8)):
            for c in range(NCH):
                for sub in range(3):
                    o0 = sub * 16

                    def store(e, f8b, dstw=dstw, c=c, o0=o0):
                        return e.dma_start(
                            out=dstw[o0:o0 + 16, c].rearrange("o p w -> p o w"),
                            in_=f8b.rearrange("p (o w) -> p o w", w=128))
                    prep_q(srcw[c * 128:(c + 1) * 128,
                                sub * 2048:(sub + 1) * 2048], store, qi)
                    qi += 1
        for o in range(NO):
            def store(e, f8b, o=o):
                return e.dma_start(
                    out=d8[:, o].rearrange("d p w -> p d w"),
                    in_=f8b.rearrange("p (d w) -> p d w", w=128))
            prep_q(ap["down_w"][o * 128:(o + 1) * 128, :], store, qi)
            qi += 1
        g_after_ffn8 = p.cnt("g")

        def prep_attn(codes, sx, store_fn):
            for c in range(NCH):
                par = c % 2
                xc = X2[:, par * 4096:par * 4096 + 2048]
                xs = X2[:, par * 4096 + 2048:par * 4096 + 4096]
                dq = A1[:, par * 2048:(par + 1) * 2048]

                def _(e, c=c, xc=xc, xs=xs, par=par):
                    p.guard(e, f"X2h{par}")
                    p.guard(e, f"X2q{2 * par}")
                    p.guard(e, f"X2q{2 * par + 1}")
                    p.inc(e.dma_start(out=xc,
                                      in_=codes[c * 128:(c + 1) * 128, :]), "ld", 16)
                    p.inc(e.dma_start(out=xs,
                                      in_=sx[c * 128:(c + 1) * 128, :]), "ld", 16)
                body("sync", _)
                ldv = p.cnt("ld")

                def _(e, c=c, ldv=ldv, xc=xc, xs=xs, dq=dq, par=par):
                    p.wait(e, "ld", ldv)
                    p.guard(e, f"A1h{par}")
                    ins = e.tensor_tensor(dq, xc, xs, OP.mult)
                    p.inc(ins, "v")
                body("vector", _)
                vv = p.cnt("v")
                p.free[f"X2h{par}"] = ("v", vv)

                def _(e, c=c, vv=vv, dq=dq):
                    p.wait(e, "v", vv)
                    ins = store_fn(e, c, dq)
                    p.inc(ins, "sc", 16)
                body("scalar", _)
                p.free[f"A1h{par}"] = ("sc", p.cnt("sc"))

        prep_attn(ap["wq_codes"], ap["wq_sx"],
                  lambda e, c, dq: e.dma_start(
                      out=wqb[:, c].rearrange("f p w -> p f w"),
                      in_=dq.rearrange("p (f w) -> p f w", w=128)))
        prep_attn(ap["wk_codes"], ap["wk_sx"],
                  lambda e, c, dq: e.dma_start(
                      out=wkb[:, c].rearrange("f p w -> p f w"),
                      in_=dq.rearrange("p (f w) -> p f w", w=128)))
        prep_attn(ap["wv_codes"], ap["wv_sx"],
                  lambda e, c, dq: e.dma_start(
                      out=wvb[c * 128:(c + 1) * 128, :], in_=dq))
        prep_attn(ap["wo_codes"], ap["wo_sx"],
                  lambda e, c, dq: e.dma_start(
                      out=wob[:, c].rearrange("d p w -> p d w"),
                      in_=dq.rearrange("p (d w) -> p d w", w=128)))
        p.rel("A1", "sc")
        p.rel("X2", "v")
        g_after_p0 = p.cnt("g")
        sc_after_p0 = p.cnt("sc")

        def dummy_out(x_sem):
            for tcx in range(4):
                def _(e, tcx=tcx, x_sem=x_sem):
                    p.wait(e, x_sem[0], x_sem[1])
                    ins = e.dma_start(
                        out=out_t[:, tcx * TB:(tcx + 1) * TB]
                        .rearrange("(c p) t -> p c t", p=128),
                        in_=X2[:, :].rearrange("p (c t) -> p c t", t=TB))
                    p.inc(ins, "g", 16)
                body("gpsimd", _)
            gf = p.cnt("g")

            def _(e, gf=gf):
                p.wait(e, "g", gf)
            body("gpsimd", _)

        if stop == 'p0':
            dummy_out(("g", g_after_p0))
            return nc

        def emit_ln(x_ready):
            for c in range(NCH):
                slot = SCR[:, (c % 2) * 512:(c % 2) * 512 + 512]

                def _(e, c=c, slot=slot):
                    if c == 0:
                        p.wait(e, x_ready[0], x_ready[1])
                    p.guard(e, f"SCR{c % 2}")
                    ins = e.activation(slot, X2[:, c * TB:(c + 1) * TB], AF.Square)
                    p.inc(ins, "s")
                body("scalar", _)
                sv = p.cnt("s")

                def _(e, c=c, slot=slot, sv=sv, x_ready=x_ready):
                    if c == 0:
                        p.wait(e, x_ready[0], x_ready[1])
                        p.guard(e, "pR")
                        p.guard(e, "pN")
                    p.wait(e, "s", sv)
                    e.matmul(pR[0:1, :], CST[:, C_ONES:C_ONES + 1],
                             X2[:, c * TB:(c + 1) * TB],
                             start=(c == 0), stop=(c == NCH - 1))
                    ins = e.matmul(pN[0:1, :], CST[:, C_ONES:C_ONES + 1], slot,
                                   start=(c == 0), stop=(c == NCH - 1))
                    p.inc(ins, "t")
                body("tensor", _)
                p.free[f"SCR{c % 2}"] = ("t", p.cnt("t"))
            t_stats = p.cnt("t")

            def _(e):
                p.wait(e, "t", t_stats)
                e.tensor_scalar(ROWS[0:1, 0:512], pR[0:1, :], 1.0 / D, None, OP.mult)
                e.tensor_scalar(ROWS[0:1, 512:1024], pN[0:1, :], 1.0 / D, None, OP.mult)
                e.tensor_tensor(ROWS[0:1, 1536:2048], ROWS[0:1, 0:512],
                                ROWS[0:1, 0:512], OP.mult)
                e.tensor_tensor(ROWS[0:1, 1024:1536], ROWS[0:1, 512:1024],
                                ROWS[0:1, 1536:2048], OP.subtract)
                ins = e.tensor_scalar(ROWS[0:1, 1024:1536], ROWS[0:1, 1024:1536],
                                      EPS, None, OP.add)
                p.inc(ins, "v")
            body("vector", _)
            p.rel("pR", "v")
            p.rel("pN", "v")
            vv = p.cnt("v")

            def _(e, vv=vv):
                p.wait(e, "v", vv)
                ins = e.activation(ROWS[0:1, 512:1024], ROWS[0:1, 1024:1536],
                                   AF.Sqrt)
                p.inc(ins, "s")
            body("scalar", _)
            sv = p.cnt("s")

            def _(e, sv=sv):
                p.wait(e, "s", sv)
                e.reciprocal(ROWS[0:1, 1536:2048], ROWS[0:1, 512:1024])
                e.tensor_tensor(RSUM[0:1, :], ROWS[0:1, 1536:2048],
                                ROWS[0:1, 1536:2048], OP.mult)
                e.tensor_tensor(RSUM[0:1, :], RSUM[0:1, :],
                                ROWS[0:1, 1024:1536], OP.mult)
                e.tensor_scalar(RSUM[0:1, :], RSUM[0:1, :], -0.5, 1.5,
                                OP.mult, OP.add)
                ins = e.tensor_tensor(ROWS[0:1, 512:1024], ROWS[0:1, 1536:2048],
                                      RSUM[0:1, :], OP.mult)
                p.inc(ins, "v")
            body("vector", _)
            vv = p.cnt("v")

            def _(e, vv=vv):
                p.wait(e, "v", vv)
                p.guard(e, "pM")
                p.guard(e, "pE")
                e.matmul(pM[:, :], ROW1[0:1, :], ROWS[0:1, 0:512],
                         start=True, stop=True)
                ins = e.matmul(pE[:, :], ROW1[0:1, :], ROWS[0:1, 512:1024],
                               start=True, stop=True)
                p.inc(ins, "t")
            body("tensor", _)
            tv = p.cnt("t")

            def _(e, tv=tv):
                p.wait(e, "t", tv)
                e.tensor_copy(MRB[:, 0:512], pM[:, :])
                ins = e.tensor_copy(MRB[:, 512:1024], pE[:, :])
                p.inc(ins, "v")
                p.guard(e, "A1")
                p.guard(e, "A2")
                for c in range(NCH):
                    e.tensor_tensor(SCR[:, 0:512], X2[:, c * TB:(c + 1) * TB],
                                    MRB[:, 0:512], OP.subtract)
                    e.tensor_tensor(SCR[:, 512:1024], SCR[:, 0:512],
                                    MRB[:, 512:1024], OP.mult)
                    e.tensor_copy(A1[:, c * TB:(c + 1) * TB], SCR[:, 512:1024])
                    ins = e.tensor_tensor(A2[:, c * TB:(c + 1) * TB],
                                          SCR[:, 512:1024],
                                          A1[:, c * TB:(c + 1) * TB], OP.subtract)
                p.inc(ins, "v")
            body("vector", _)
            p.rel("pM", "v")
            p.rel("pE", "v")
            return p.cnt("v")

        def emit_ffn(x_ready, w_ready_g):
            v_yt = emit_ln(x_ready)
            if ffn_mode == 'ln':
                return v_yt
            gbanks = [(pA, "pA"), (pC, "pC")]
            ubanks = [(pB, "pB"), (pD, "pD")]
            for q in range(4):
                for oi in range(OQ):
                    o = q * OQ + oi
                    if o >= olim:
                        continue
                    par = o % 2
                    gslot = A5[:, par * 4096:par * 4096 + 2048]
                    uslot = A5[:, par * 4096 + 2048:par * 4096 + 4096]

                    def _(e, o=o, par=par, gslot=gslot, uslot=uslot,
                          w_ready_g=w_ready_g):
                        if o == 0:
                            p.wait(e, "g", w_ready_g)
                        for k_ in (("A5q0", "A5q1") if par == 0 else ("A5q2", "A5q3")):
                            p.guard(e, k_)
                        p.inc(e.dma_start(
                            out=gslot.rearrange("p (c w) -> p c w", w=128),
                            in_=g8[o].rearrange("c p w -> p c w")), "g", 16)
                        p.inc(e.dma_start(
                            out=uslot.rearrange("p (c w) -> p c w", w=128),
                            in_=u8[o].rearrange("c p w -> p c w")), "g", 16)
                    body("gpsimd", _)
                    gv = p.cnt("g")

                    def _(e, o=o, par=par, gslot=gslot, uslot=uslot, gv=gv,
                          v_yt=v_yt):
                        p.wait(e, "g", gv)
                        if o == 0:
                            p.wait(e, "v", v_yt)
                        gp, gn = gbanks[par]
                        up, un = ubanks[par]
                        p.guard(e, gn)
                        p.guard(e, un)
                        for c in range(NCH):
                            lt = gslot[:, c * 128:(c + 1) * 128]
                            e.matmul(gp[:, :], lt, A1[:, c * TB:(c + 1) * TB],
                                     start=(c == 0), stop=False)
                            ins = e.matmul(gp[:, :], lt, A2[:, c * TB:(c + 1) * TB],
                                           start=False, stop=(c == NCH - 1))
                        p.inc(ins, "t")
                        for c in range(NCH):
                            lt = uslot[:, c * 128:(c + 1) * 128]
                            e.matmul(up[:, :], lt, A1[:, c * TB:(c + 1) * TB],
                                     start=(c == 0), stop=False)
                            ins = e.matmul(up[:, :], lt, A2[:, c * TB:(c + 1) * TB],
                                           start=False, stop=(c == NCH - 1))
                        p.inc(ins, "t")
                    body("tensor", _)
                    for k_ in (("A5q0", "A5q1") if par == 0 else ("A5q2", "A5q3")):
                        p.free[k_] = ("t", p.cnt("t"))
                    tv = p.cnt("t")
                    if skip_sv >= 2:
                        p.free[gbanks[par][1]] = ("t", p.cnt("t"))
                        p.free[ubanks[par][1]] = ("t", p.cnt("t"))
                        continue

                    def _(e, o=o, par=par, tv=tv):
                        p.wait(e, "t", tv)
                        p.guard(e, f"SCR{par}")
                        gp, _gn = gbanks[par]
                        sil = SCR[:, par * 512:par * 512 + 512]
                        ins = e.activation(sil, gp[:, :], AF.Silu,
                                           scale=CST[:, C_GS + o:C_GS + o + 1])
                        p.inc(ins, "s")
                    body("scalar", _)
                    p.free[gbanks[par][1]] = ("s", p.cnt("s"))
                    sv = p.cnt("s")
                    if skip_sv >= 1:
                        continue

                    def _(e, o=o, oi=oi, par=par, sv=sv):
                        p.wait(e, "s", sv)
                        p.guard(e, f"SCR{2 + par}")
                        if oi == 0:
                            p.guard(e, "SCR4")
                            for k_ in ("A3", "A4w0", "A4w1", "A4r"):
                                p.guard(e, k_)
                        up, _un = ubanks[par]
                        sil = SCR[:, par * 512:par * 512 + 512]
                        upz = SCR[:, 1024 + par * 512:1024 + par * 512 + 512]
                        ff = SCR[:, 2048:2560]
                        e.tensor_scalar(upz, up[:, :],
                                        CST[:, C_US + o:C_US + o + 1], None,
                                        OP.mult)
                        e.tensor_tensor(ff, sil, upz, OP.mult)
                        e.tensor_copy(A3[:, oi * 512:(oi + 1) * 512], ff)
                        ins = e.tensor_tensor(A4[:, oi * 512:(oi + 1) * 512],
                                              ff, A3[:, oi * 512:(oi + 1) * 512],
                                              OP.subtract)
                        p.inc(ins, "v")
                    body("vector", _)
                    p.free[ubanks[par][1]] = ("v", p.cnt("v"))
                    p.free[f"SCR{par}"] = ("v", p.cnt("v"))
                    p.free[f"SCR{2 + par}"] = ("v", p.cnt("v"))
                v_fuse = p.cnt("v")
                if ffn_mode == 'gu':
                    for k_ in ("A3", "A4w0", "A4w1", "A4r"):
                        p.free[k_] = ("v", p.cnt("v"))
                    continue

                dbanks = [(pA, "pA"), (pB, "pB")]
                for dg in range(8):
                    par = dg % 2
                    dwslot = A6[:, par * 3072:par * 3072 + 3072]

                    def _(e, q=q, dg=dg, par=par, dwslot=dwslot):
                        for j_ in range(6):
                            p.guard(e, f"A6_{par * 6 + j_}")
                        for dd in range(2):
                            p.inc(e.dma_start(
                                out=dwslot[:, dd * 1536:(dd + 1) * 1536]
                                .rearrange("p (o w) -> p o w", w=128),
                                in_=d8[dg * 2 + dd, q * OQ:(q + 1) * OQ]
                                .rearrange("o p w -> p o w")), "g", 16)
                    body("gpsimd", _)
                    gv = p.cnt("g")

                    for dd in range(2):
                        d = dg * 2 + dd
                        bank, bname = dbanks[d % 2]

                        def _(e, q=q, d=d, dd=dd, par=par, dwslot=dwslot,
                              gv=gv, bank=bank, bname=bname, v_fuse=v_fuse):
                            if dd == 0:
                                p.wait(e, "g", gv)
                                if d == 0:
                                    p.wait(e, "v", v_fuse)
                            p.guard(e, bname)
                            for oi in range(OQ):
                                lt = dwslot[:, dd * 1536 + oi * 128:
                                            dd * 1536 + (oi + 1) * 128]
                                e.matmul(bank[:, :], lt,
                                         A3[:, oi * 512:(oi + 1) * 512],
                                         start=(oi == 0), stop=False)
                                ins = e.matmul(bank[:, :], lt,
                                               A4[:, oi * 512:(oi + 1) * 512],
                                               start=False, stop=(oi == OQ - 1))
                            p.inc(ins, "t")
                        body("tensor", _)
                        tv = p.cnt("t")

                        def _(e, d=d, bank=bank, tv=tv):
                            p.wait(e, "t", tv)
                            ins = e.scalar_tensor_tensor(
                                X2[:, d * TB:(d + 1) * TB], bank[:, :],
                                CST[:, C_DSH + d:C_DSH + d + 1],
                                X2[:, d * TB:(d + 1) * TB], OP.mult, OP.add)
                            p.inc(ins, "v")
                        body("vector", _)
                        p.free[bname] = ("v", p.cnt("v"))
                    for j_ in range(6):
                        p.free[f"A6_{par * 6 + j_}"] = ("t", p.cnt("t"))
                for k_ in ("A3", "A4w0", "A4w1", "A4r"):
                    p.free[k_] = ("t", p.cnt("t"))
            p.rel("A1", "v")
            p.rel("A2", "v")
            return p.cnt("v")

        x1_store_g = []
        for tb in range(p1_tbs):
            def _(e, tb=tb):
                p.guard(e, "X2")
                ins = e.dma_start(
                    out=X2[:, :].rearrange("p (c t) -> p c t", t=TB),
                    in_=ap["x0t"][:, tb * TB:(tb + 1) * TB]
                    .rearrange("(c p) t -> p c t", p=128))
                p.inc(ins, "ld", 16)
            body("sync", _)
            xv = p.cnt("ld")
            v_done = emit_ffn(("ld", xv), g_after_ffn8)

            def _(e, tb=tb, v_done=v_done):
                p.wait(e, "v", v_done)
                ins = e.dma_start(
                    out=x1t[:, tb * TB:(tb + 1) * TB]
                    .rearrange("(c p) t -> p c t", p=128),
                    in_=X2[:, :].rearrange("p (c t) -> p c t", t=TB))
                p.inc(ins, "g", 16)
            body("gpsimd", _)
            x1_store_g.append(p.cnt("g"))
            p.rel("X2", "g")

        if stop == 'p1':
            dummy_out(("g", p.cnt("g")))
            return nc

        kq_banks = [(pA, "pA"), (pB, "pB")]
        for tc in range(p2_tcs):
            def _(e, tc=tc):
                p.guard(e, "X2")
                p.wait(e, "g", x1_store_g[min(tc, len(x1_store_g) - 1)])
                ins = e.dma_start(
                    out=X2[:, :].rearrange("p (c t) -> p c t", t=TB),
                    in_=x1t[:, tc * TB:(tc + 1) * TB]
                    .rearrange("(c p) t -> p c t", p=128))
                p.inc(ins, "ld", 16)
            body("sync", _)
            xv = p.cnt("ld")
            v_yt = emit_ln(("ld", xv))

            projs = [("k", wkb, C_BK, kt_hi, kt_lo, KN, tc * TB)]
            if tc < 4:
                projs.append(("q", wqb, C_BQ, qt_hi, qt_lo, QN, tc * TB))
            for pname, wtile, cbias, dhi, dlo, nmax, coff in projs:
                for f in range(H):
                    par = f % 2
                    wpar = f % 4
                    wslot = A5[:, wpar * 2048:wpar * 2048 + 2048]

                    def _(e, f=f, wpar=wpar, wslot=wslot, wtile=wtile):
                        p.guard(e, f"A5q{wpar}")
                        p.wait(e, "sc", sc_after_p0)
                        ins = e.dma_start(
                            out=wslot.rearrange("p (c w) -> p c w", w=128),
                            in_=wtile[f].rearrange("c p w -> p c w"))
                        p.inc(ins, "ld", 16)
                    body("sync", _)
                    ldv = p.cnt("ld")
                    bank, bname = kq_banks[par]

                    def _(e, f=f, par=par, wslot=wslot, ldv=ldv, bank=bank,
                          bname=bname, v_yt=v_yt):
                        p.wait(e, "ld", ldv)
                        if f == 0:
                            p.wait(e, "v", v_yt)
                        p.guard(e, bname)
                        for c in range(NCH):
                            lt = wslot[:, c * 128:(c + 1) * 128]
                            e.matmul(bank[:, :], lt, A1[:, c * TB:(c + 1) * TB],
                                     start=(c == 0), stop=False)
                            ins = e.matmul(bank[:, :], lt, A2[:, c * TB:(c + 1) * TB],
                                           start=False, stop=(c == NCH - 1))
                        p.inc(ins, "t")
                    body("tensor", _)
                    p.free[f"A5q{wpar}"] = ("t", p.cnt("t"))
                    tv = p.cnt("t")

                    kslot = (2 + 2 * (f % 4)) * 512 if f % 4 < 2 else \
                        (8 + 2 * (f % 4 - 2)) * 512

                    def _(e, f=f, par=par, bank=bank, tv=tv, cbias=cbias,
                          kslot=kslot):
                        p.wait(e, "t", tv)
                        p.guard(e, f"SCR{par}")
                        kf = SCR[:, 0:512] if par == 0 else SCR[:, 512:1024]
                        ins = e.activation(kf, bank[:, :], AF.Identity,
                                           bias=CST[:, cbias + f:cbias + f + 1])
                        p.inc(ins, "s")
                    body("scalar", _)
                    p.free[bname] = ("s", p.cnt("s"))
                    sv = p.cnt("s")

                    def _(e, f=f, par=par, sv=sv, kslot=kslot):
                        p.wait(e, "s", sv)
                        p.guard(e, f"A6_{par}")
                        kf = SCR[:, 0:512] if par == 0 else SCR[:, 512:1024]
                        ins = e.tensor_tensor(A6[:, par * 512:par * 512 + 512],
                                              kf, kf, OP.mult)
                        p.inc(ins, "v")
                        p.guard(e, f"A6_{kslot // 512}")
                        p.guard(e, f"A6_{kslot // 512 + 1}")
                        e.tensor_copy(A6[:, kslot:kslot + 512], kf)
                        ins = e.tensor_tensor(A6[:, kslot + 512:kslot + 1024],
                                              kf, A6[:, kslot:kslot + 512],
                                              OP.subtract)
                        p.inc(ins, "v")
                    body("vector", _)
                    p.free[f"SCR{par}"] = ("v", p.cnt("v"))
                    vv = p.cnt("v")

                    nb, nbk = (pN, "pN") if f % 2 == 0 else (pR, "pR")

                    def _(e, f=f, par=par, vv=vv, nb=nb, nbk=nbk):
                        p.wait(e, "v", vv - 1)
                        p.guard(e, nbk)
                        ins = e.matmul(nb[0:1, :], ONESB[:, 0:1],
                                       A6[:, par * 512:par * 512 + 512],
                                       start=True, stop=True)
                        p.inc(ins, "t")
                    body("tensor", _)
                    p.free[f"A6_{par}"] = ("t", p.cnt("t"))
                    tnv = p.cnt("t")

                    def _(e, f=f, tc=tc, tnv=tnv, nmax=nmax, nb=nb):
                        p.wait(e, "t", tnv)
                        if tc == 0:
                            ins = e.tensor_reduce(nmax[0:1, f:f + 1], nb[0:1, :],
                                                  AX, OP.max)
                        else:
                            e.tensor_reduce(TMP1[0:1, 0:1], nb[0:1, :], AX, OP.max)
                            ins = e.tensor_tensor(nmax[0:1, f:f + 1],
                                                  nmax[0:1, f:f + 1],
                                                  TMP1[0:1, 0:1], OP.max)
                        p.inc(ins, "v")
                    body("vector", _)
                    p.free[nbk] = ("v", p.cnt("v"))

                    def _(e, f=f, par=par, vv=vv, dhi=dhi, dlo=dlo, coff=coff,
                          kslot=kslot):
                        p.wait(e, "v", vv)
                        p.inc(e.dma_start(out=dhi[f][:, coff:coff + 512],
                                          in_=A6[:, kslot:kslot + 512]), "g", 16)
                        p.inc(e.dma_start(out=dlo[f][:, coff:coff + 512],
                                          in_=A6[:, kslot + 512:kslot + 1024]),
                              "g", 16)
                    body("gpsimd", _)
                    p.free[f"A6_{kslot // 512}"] = ("g", p.cnt("g"))
                    p.free[f"A6_{kslot // 512 + 1}"] = ("g", p.cnt("g"))

            v_banks = [(pC, "pC"), (pD, "pD")]
            for fb in range(4):
                for half in range(2):
                    def _(e, tc=tc, fb=fb, half=half):
                        p.guard(e, f"A5q{half * 2}")
                        p.guard(e, f"A5q{half * 2 + 1}")
                        p.wait(e, "sc", sc_after_p0)
                        ins = e.dma_start(
                            out=A5[:, half * 4096:(half + 1) * 4096]
                            .rearrange("p (c w) -> p c w", w=512),
                            in_=wvb[half * 1024:(half + 1) * 1024,
                                    fb * 512:(fb + 1) * 512]
                            .rearrange("(c p) w -> p c w", p=128))
                        p.inc(ins, "ld", 16)
                    body("sync", _)
                ldv = p.cnt("ld")
                for tt in range(4):
                    bank, bname = v_banks[tt % 2]

                    def _(e, fb=fb, tt=tt, ldv=ldv, bank=bank, bname=bname,
                          v_yt=v_yt):
                        p.wait(e, "ld", ldv)
                        if fb == 0 and tt == 0:
                            p.wait(e, "v", v_yt)
                        p.guard(e, bname)
                        for c in range(NCH):
                            half = c // 8
                            wsl = A5[:, half * 4096 + (c % 8) * 512:
                                     half * 4096 + (c % 8) * 512 + 512]
                            lt1 = A1[:, c * TB + tt * 128:c * TB + (tt + 1) * 128]
                            lt2 = A2[:, c * TB + tt * 128:c * TB + (tt + 1) * 128]
                            e.matmul(bank[:, :], lt1, wsl,
                                     start=(c == 0), stop=False)
                            ins = e.matmul(bank[:, :], lt2, wsl,
                                           start=False, stop=(c == NCH - 1))
                        p.inc(ins, "t")
                    body("tensor", _)
                    if tt == 3:
                        for k_ in ("A5q0", "A5q1", "A5q2", "A5q3"):
                            p.free[k_] = ("t", p.cnt("t"))
                    tv = p.cnt("t")
                    vslot = 2048 + (tt % 2) * 2048

                    def _(e, fb=fb, tt=tt, tv=tv, bank=bank, vslot=vslot):
                        p.wait(e, "t", tv)
                        p.guard(e, f"A6_{4 + 4 * (tt % 2)}")
                        p.guard(e, f"A6_{5 + 4 * (tt % 2)}")
                        vf = SCR[:, 2048:2560]
                        e.tensor_tensor(vf, bank[:, :],
                                        BVB[:, fb * 512:(fb + 1) * 512], OP.add)
                        e.tensor_copy(A6[:, vslot:vslot + 512], vf)
                        ins = e.tensor_tensor(A6[:, vslot + 512:vslot + 1024],
                                              vf, A6[:, vslot:vslot + 512],
                                              OP.subtract)
                        p.inc(ins, "v")
                    body("vector", _)
                    p.free[bname] = ("v", p.cnt("v"))
                    vv = p.cnt("v")

                    def _(e, tc=tc, fb=fb, tt=tt, vv=vv, vslot=vslot):
                        p.wait(e, "v", vv)
                        tile = tc * 4 + tt
                        for j in range(4):
                            h = fb * 4 + j
                            p.inc(e.dma_start(
                                out=v_hi[h][:, tile, :],
                                in_=A6[:, vslot + j * 128:vslot + (j + 1) * 128]),
                                "g", 16)
                    body("gpsimd", _)
                    p.free[f"A6_{4 + 4 * (tt % 2)}"] = ("g", p.cnt("g"))

                    def _(e, tc=tc, fb=fb, tt=tt, vv=vv, vslot=vslot):
                        p.wait(e, "v", vv)
                        tile = tc * 4 + tt
                        for j in range(4):
                            h = fb * 4 + j
                            p.inc(e.dma_start(
                                out=v_lo[h][:, tile, :],
                                in_=A6[:, vslot + 512 + j * 128:
                                       vslot + 512 + (j + 1) * 128]), "sc", 16)
                    body("scalar", _)
                    p.free[f"A6_{5 + 4 * (tt % 2)}"] = ("sc", p.cnt("sc"))
            p.rel("X2", "v")
            p.rel("A1", "t")
            p.rel("A2", "t")
        g_after_p2 = p.cnt("g")
        sc_after_p2 = p.cnt("sc")
        ld_after_p2 = p.cnt("ld")

        if stop == 'p2':
            dummy_out(("sc", p.cnt("sc")))
            return nc

        def _(e):
            p.wait(e, "v", p.cnt("v"))
            ins = e.tensor_tensor(TMP1[0:1, 0:H], QN[0:1, :], KN[0:1, :], OP.mult)
            p.inc(ins, "v")
        body("vector", _)
        vv = p.cnt("v")

        def _(e, vv=vv):
            p.wait(e, "v", vv)
            ins = e.activation(M16[0:1, :], TMP1[0:1, 0:H], AF.Sqrt)
            p.inc(ins, "s")
        body("scalar", _)
        sv = p.cnt("s")

        def _(e, sv=sv):
            p.wait(e, "s", sv)
            ins = e.tensor_scalar(M16[0:1, :], M16[0:1, :],
                                  -1.01 * SCALE, None, OP.mult)
            p.inc(ins, "v")
        body("vector", _)
        vv = p.cnt("v")
        for hh in range(H):
            def _(e, hh=hh, vv=vv):
                if hh == 0:
                    p.wait(e, "v", vv)
                p.guard(e, "pM")
                ins = e.matmul(pM[:, 0:1], ROW1[0:1, :], M16[0:1, hh:hh + 1],
                               start=True, stop=True)
                p.inc(ins, "t")
            body("tensor", _)
            tv = p.cnt("t")

            def _(e, hh=hh, tv=tv):
                p.wait(e, "t", tv)
                ins = e.tensor_copy(MCOL[:, hh:hh + 1], pM[:, 0:1])
                p.inc(ins, "v")
            body("vector", _)
            p.rel("pM", "v")

        st_banks = [(pA, "pA"), (pB, "pB")]
        for tc in range(f_tcs):
            def _(e, tc=tc):
                p.guard(e, "X2")
                p.wait(e, "g", x1_store_g[min(tc, len(x1_store_g) - 1)])
                ins = e.dma_start(
                    out=X2[:, :].rearrange("p (c t) -> p c t", t=TB),
                    in_=x1t[:, tc * TB:(tc + 1) * TB]
                    .rearrange("(c p) t -> p c t", p=128))
                p.inc(ins, "ld", 16)
            body("sync", _)
            xv = p.cnt("ld")

            def _(e, tc=tc, xv=xv):
                p.wait(e, "ld", xv)
                ins = None
                for c in range(NCH):
                    ins = e.activation(X2[:, c * TB:(c + 1) * TB],
                                       X2[:, c * TB:(c + 1) * TB], AF.Identity,
                                       bias=CST[:, C_BO + c:C_BO + c + 1])
                p.inc(ins, "s")
            body("scalar", _)
            s_x2 = p.cnt("s")
            p.free["X2p"] = ("s", s_x2)

            for h in range(f_hs):
                g_ = h // 4
                def _(e, tc=tc, h=h):
                    p.guard(e, "A1")
                    p.guard(e, "A5q0")
                    p.wait(e, "g", g_after_p2)
                    p.inc(e.dma_start(out=A1[:, 0:4096], in_=kt_hi[h]), "ld", 16)
                    p.inc(e.dma_start(out=A1[:, 4096:8192], in_=kt_lo[h]), "ld", 16)
                    p.inc(e.dma_start(out=A5[:, 0:512],
                                      in_=qt_hi[h][:, tc * 512:(tc + 1) * 512]),
                          "ld", 16)
                    p.inc(e.dma_start(out=A5[:, 512:1024],
                                      in_=qt_lo[h][:, tc * 512:(tc + 1) * 512]),
                          "ld", 16)
                body("sync", _)
                ldv = p.cnt("ld")

                def _(e, tc=tc, h=h):
                    p.guard(e, "A2")
                    p.wait(e, "sc", sc_after_p2)
                    p.wait(e, "g", g_after_p2)
                    p.inc(e.dma_start(
                        out=A2[:, 0:4096],
                        in_=v_hi[h].rearrange("p T w -> p (T w)")), "sc", 16)
                    p.inc(e.dma_start(
                        out=A2[:, 4096:8192],
                        in_=v_lo[h].rearrange("p T w -> p (T w)")), "sc", 16)
                body("scalar", _)
                scv = p.cnt("sc")

                st4 = [(pA, "pA"), (pB, "pB"), (pD, "pD"), (pE, "pE")]
                RACC = SCR[:, 2048:2560]
                OF = SCR[:, 2560:3072]

                def emit_st(e, sq, first_ld=None):
                    bank, bname = st4[sq % 4]
                    if first_ld is not None:
                        p.wait(e, "ld", first_ld)
                    p.guard(e, bname)
                    kh = A1[:, sq * 128:(sq + 1) * 128]
                    kl = A1[:, 4096 + sq * 128:4096 + (sq + 1) * 128]
                    e.matmul(bank[:, :], kh, A5[:, 0:512],
                             start=True, stop=False)
                    e.matmul(bank[:, :], kh, A5[:, 512:1024],
                             start=False, stop=False)
                    ins = e.matmul(bank[:, :], kl, A5[:, 0:512],
                                   start=False, stop=True)
                    p.inc(ins, "t")

                def _(e, h=h, ldv=ldv, scv=scv):
                    p.guard(e, "pC")
                    p.wait(e, "sc", scv)
                    for sq in range(4):
                        emit_st(e, sq, first_ld=(ldv if sq == 0 else None))
                body("tensor", _)
                st_tv = [p.cnt("t") - 3 + i for i in range(4)]

                pv_pending = None
                for sq in range(32):
                    par = sq % 4
                    bank, bname = st4[par]
                    tv = st_tv[sq]

                    def _(e, h=h, sq=sq, par=par, bank=bank, tv=tv):
                        p.wait(e, "t", tv)
                        p.guard(e, f"SCR{par}")
                        p.guard(e, f"A6_{par}")
                        pf = SCR[:, par * 512:par * 512 + 512]
                        e.activation(pf, bank[:, :], AF.Exp,
                                     bias=MCOL[:, h:h + 1], scale=SCALE)
                        ins = e.activation(A6[:, par * 512:par * 512 + 512],
                                           pf, AF.Exp,
                                           bias=MCOL[:, h:h + 1], scale=SCALE) \
                            if False else e.copy(
                                A6[:, par * 512:par * 512 + 512], pf)
                        p.inc(ins, "s")
                    body("scalar", _)
                    p.free[bname] = ("s", p.cnt("s"))
                    sv = p.cnt("s")

                    def _(e, sq=sq, par=par, sv=sv):
                        p.wait(e, "s", sv)
                        p.guard(e, f"A6_{4 + par}")
                        pf = SCR[:, par * 512:par * 512 + 512]
                        if sq == 0:
                            p.guard(e, "SCR4")
                            e.tensor_copy(RACC, pf)
                        else:
                            e.tensor_tensor(RACC, RACC, pf, OP.add)
                        ins = e.tensor_tensor(
                            A6[:, 2048 + par * 512:2048 + par * 512 + 512], pf,
                            A6[:, par * 512:par * 512 + 512], OP.subtract)
                        p.inc(ins, "v")
                    body("vector", _)
                    p.free[f"SCR{par}"] = ("v", p.cnt("v"))
                    vcnt = p.cnt("v")

                    def _(e, sq=sq, par=par, vcnt=vcnt, h=h):
                        p.wait(e, "v", vcnt)
                        vh = A2[:, sq * 128:(sq + 1) * 128]
                        vl = A2[:, 4096 + sq * 128:4096 + (sq + 1) * 128]
                        ph = A6[:, par * 512:par * 512 + 512]
                        pl = A6[:, 2048 + par * 512:2048 + par * 512 + 512]
                        e.matmul(pC[:, :], vh, ph, start=(sq == 0), stop=False)
                        e.matmul(pC[:, :], vh, pl, start=False, stop=False)
                        ins = e.matmul(pC[:, :], vl, ph, start=False,
                                       stop=(sq == 31))
                        p.inc(ins, "t")
                        if sq < 28:
                            emit_st(e, sq + 4)
                    body("tensor", _)
                    p.free[f"A6_{par}"] = ("t", p.cnt("t"))
                    p.free[f"A6_{4 + par}"] = ("t", p.cnt("t"))
                    if sq < 28:
                        st_tv.append(p.cnt("t"))

                p.free["A5q0"] = ("t", p.cnt("t"))
                p.free["A1"] = ("t", p.cnt("t"))
                p.free["A2"] = ("t", p.cnt("t"))

                def _(e):
                    p.guard(e, "pR")
                    ins = e.matmul(pR[0:1, :], CST[:, C_ONES:C_ONES + 1],
                                   RACC, start=True, stop=True)
                    p.inc(ins, "t")
                body("tensor", _)
                p.free["SCR4"] = ("t", p.cnt("t"))
                tv = p.cnt("t")

                def _(e, tv=tv):
                    p.wait(e, "t", tv)
                    ins = e.reciprocal(RSUM[0:1, :], pR[0:1, :])
                    p.inc(ins, "v")
                body("vector", _)
                p.rel("pR", "v")
                vv = p.cnt("v")

                def _(e, vv=vv):
                    p.wait(e, "v", vv)
                    p.guard(e, "pM")
                    ins = e.matmul(pM[:, :], ROW1[0:1, :], RSUM[0:1, :],
                                   start=True, stop=True)
                    p.inc(ins, "t")
                body("tensor", _)
                tv = p.cnt("t")

                def _(e, h=h, tv=tv):
                    p.wait(e, "t", tv)
                    p.guard(e, "A3")
                    e.tensor_copy(MRB[:, 0:512], pM[:, :])
                    of = SCR[:, 2560:3072]
                    e.tensor_tensor(of, pC[:, :], MRB[:, 0:512], OP.mult)
                    hh = h % 4
                    e.tensor_copy(A3[:, hh * 512:(hh + 1) * 512], of)
                    ins = e.tensor_tensor(A3[:, 2048 + hh * 512:
                                             2048 + (hh + 1) * 512], of,
                                          A3[:, hh * 512:(hh + 1) * 512],
                                          OP.subtract)
                    p.inc(ins, "v")
                body("vector", _)
                p.rel("pM", "v")
                p.rel("pC", "v")
                p.free["SCR3"] = ("v", p.cnt("v"))
                v_og = p.cnt("v")

                if h % 4 == 3:
                    pj_banks = [(pD, "pD"), (pE, "pE")]
                    for c in range(NCH):
                        par = c % 2

                        def _(e, tc=tc, h=h, c=c, par=par):
                            p.guard(e, f"A4w{par}")
                            p.wait(e, "sc", sc_after_p0)
                            ins = e.dma_start(
                                out=A4[:, par * 512:par * 512 + 512]
                                .rearrange("p (j w) -> p j w", w=128),
                                in_=wob[c, (h - 3):(h + 1)]
                                .rearrange("j p w -> p j w"))
                            p.inc(ins, "ld", 16)
                        body("sync", _)
                        ldv = p.cnt("ld")
                        bank, bname = pj_banks[par]

                        def _(e, c=c, par=par, ldv=ldv, bank=bank, bname=bname,
                              v_og=v_og):
                            p.wait(e, "ld", ldv)
                            if c == 0:
                                p.wait(e, "v", v_og)
                            p.guard(e, bname)
                            for j in range(4):
                                wt = A4[:, par * 512 + j * 128:
                                        par * 512 + (j + 1) * 128]
                                e.matmul(bank[:, :], wt,
                                         A3[:, j * 512:(j + 1) * 512],
                                         start=(j == 0), stop=False)
                                ins = e.matmul(bank[:, :], wt,
                                               A3[:, 2048 + j * 512:
                                                  2048 + (j + 1) * 512],
                                               start=False, stop=(j == 3))
                            p.inc(ins, "t")
                        body("tensor", _)
                        p.free[f"A4w{par}"] = ("t", p.cnt("t"))
                        tv = p.cnt("t")

                        def _(e, c=c, tv=tv, bank=bank, s_x2=s_x2):
                            p.wait(e, "t", tv)
                            if c == 0:
                                p.wait(e, "s", s_x2)
                            ins = e.tensor_tensor(X2[:, c * TB:(c + 1) * TB],
                                                  X2[:, c * TB:(c + 1) * TB],
                                                  bank[:, :], OP.add)
                            p.inc(ins, "v")
                        body("vector", _)
                        p.free[bname] = ("v", p.cnt("v"))
                    p.free["A3"] = ("t", p.cnt("t"))

            if stop == 'flash':
                v_done = p.cnt("v")
            else:
                v_done = emit_ffn(("v", p.cnt("v")), g_after_p0)

            def _(e, tc=tc, v_done=v_done):
                p.wait(e, "v", v_done)
                ins = e.dma_start(
                    out=out_t[:, tc * TB:(tc + 1) * TB]
                    .rearrange("(c p) t -> p c t", p=128),
                    in_=X2[:, :].rearrange("p (c t) -> p c t", t=TB))
                p.inc(ins, "g", 16)
            body("gpsimd", _)
            p.rel("X2", "g")

        g_final = p.cnt("g")

        def _(e, g_final=g_final):
            p.wait(e, "g", g_final)
        body("gpsimd", _)

    return nc


_NC_CACHE = {}


def kernel(L0_attn_x, L0_attn_wq_codes, L0_attn_wq_scales, L0_attn_bq,
           L0_attn_wk_codes, L0_attn_wk_scales, L0_attn_bk,
           L0_attn_wv_codes, L0_attn_wv_scales, L0_attn_bv,
           L0_attn_wo_codes, L0_attn_wo_scales, L0_attn_bo,
           L0_ffn_gate_w, L0_ffn_up_w, L0_ffn_down_w,
           L0_ffn_gate_scale, L0_ffn_up_scale, L0_ffn_down_scale):
    x = np.asarray(L0_attn_x, dtype=np.float32)

    def col2(b):
        return np.ascontiguousarray(np.asarray(b, np.float32).reshape(NCH, 128).T)

    def col2w(b):
        return np.ascontiguousarray(np.asarray(b, np.float32).reshape(NO, 128).T)

    shared = {
        "wq_codes": np.ascontiguousarray(L0_attn_wq_codes, np.float32),
        "wk_codes": np.ascontiguousarray(L0_attn_wk_codes, np.float32),
        "wv_codes": np.ascontiguousarray(L0_attn_wv_codes, np.float32),
        "wo_codes": np.ascontiguousarray(L0_attn_wo_codes, np.float32),
        "wq_sx": np.ascontiguousarray(np.repeat(np.asarray(L0_attn_wq_scales, np.float32), 32, axis=0)),
        "wk_sx": np.ascontiguousarray(np.repeat(np.asarray(L0_attn_wk_scales, np.float32), 32, axis=0)),
        "wv_sx": np.ascontiguousarray(np.repeat(np.asarray(L0_attn_wv_scales, np.float32), 32, axis=0)),
        "wo_sx": np.ascontiguousarray(np.repeat(np.asarray(L0_attn_wo_scales, np.float32), 32, axis=0)),
        "bq2": col2(L0_attn_bq), "bk2": col2(L0_attn_bk), "bo2": col2(L0_attn_bo),
        "bv_row": np.ascontiguousarray(np.asarray(L0_attn_bv, np.float32).reshape(1, D)),
        "gate_w": np.ascontiguousarray(L0_ffn_gate_w, np.float32),
        "up_w": np.ascontiguousarray(L0_ffn_up_w, np.float32),
        "down_w": np.ascontiguousarray(L0_ffn_down_w, np.float32),
        "gs2": col2w(L0_ffn_gate_scale), "us2": col2w(L0_ffn_up_scale),
        "ds2": col2(L0_ffn_down_scale),
    }

    in_maps = []
    for core in range(8):
        b, half = core // 2, core % 2
        own = x[b, half * TOWN:(half + 1) * TOWN, :]
        other = x[b, (1 - half) * TOWN:(2 - half) * TOWN, :]
        x0t = np.ascontiguousarray(np.concatenate([own, other], axis=0).T)
        m = dict(shared)
        m["x0t"] = x0t
        in_maps.append(m)

    if "nc" not in _NC_CACHE:
        _NC_CACHE["nc"] = build()
    nc = _NC_CACHE["nc"]
    res = run_bass_kernel_spmd(nc, in_maps, core_ids=list(range(8)))

    out = np.empty((4, S, D), dtype=np.float32)
    for core in range(8):
        b, half = core // 2, core % 2
        out[b, half * TOWN:(half + 1) * TOWN, :] = res.results[core]["out_t"].T
    return out


if __name__ == "__main__":
    import time
    t0 = time.time()
    nc = build()
    print(f"build {time.time() - t0:.1f}s")


# version 12 (final)
# speedup vs baseline: 1.1112x; 1.0599x over previous
import numpy as np
from contextlib import ExitStack

import concourse.bass as bass
import concourse.mybir as mybir
from concourse.alu_op_type import AluOpType as OP
from concourse.bass_utils import run_bass_kernel_spmd

F32 = mybir.dt.float32
BF16 = mybir.dt.bfloat16
FP8 = mybir.dt.float8e4
I32 = mybir.dt.int32
AF = mybir.ActivationFunctionType
AX = mybir.AxisListType.X

D = 2048
S = 4096
TOWN = 2048
H = 16
FI = 6144
NCH = 16
NO = 48
OQ = 12
TB = 512
SCALE = float(1.0 / np.sqrt(128.0))
EPS = 1e-5

C_ONES = 0
C_BQ = 1
C_BK = C_BQ + NCH
C_BO = C_BK + NCH
C_GS = C_BO + NCH
C_US = C_GS + NO
C_DSH = C_US + NO
C_TOT = C_DSH + NCH


class P:

    def __init__(self):
        self.sem = {}
        self.free = {}

    def add(self, name, handle):
        self.sem[name] = [handle, 0]

    def cnt(self, name):
        return self.sem[name][1]

    def inc(self, ins, name, k=1):
        h = self.sem[name]
        ins.then_inc(h[0], k)
        h[1] += k
        return h[1]

    def wait(self, eng, name, val):
        h = self.sem[name]
        assert val <= h[1], f"forward wait {name}: {val} > {h[1]}"
        if val > 0:
            eng.wait_ge(h[0], val)

    def guard(self, eng, res):
        v = self.free.get(res)
        if v is not None:
            self.wait(eng, v[0], v[1])

    def rel(self, res, semname):
        self.free[res] = (semname, self.cnt(semname))


def build(stop=None, p1_tbs=8, ffn_mode='full', olim=48, skip_sv=0,
          p2_tcs=8, f_tcs=4, f_hs=16, races=True):

    nc = bass.Bass("TRN2", target_bir_lowering=False, detect_race_conditions=races)

    ap = {}
    ap["x0t"] = nc.dram_tensor("x0t", [D, S], F32, kind="ExternalInput").ap()
    for nm, shp in [
        ("wq_codes", [D, D]), ("wk_codes", [D, D]), ("wv_codes", [D, D]),
        ("wo_codes", [D, D]),
        ("wq_sx", [D, D]), ("wk_sx", [D, D]), ("wv_sx", [D, D]), ("wo_sx", [D, D]),
        ("bq2", [128, NCH]), ("bk2", [128, NCH]), ("bo2", [128, NCH]),
        ("bv_row", [1, D]),
        ("gate_w", [D, FI]), ("up_w", [D, FI]), ("down_w", [FI, D]),
        ("gs2", [128, NO]), ("us2", [128, NO]), ("ds2", [128, NCH]),
    ]:
        ap[nm] = nc.dram_tensor(nm, shp, F32, kind="ExternalInput").ap()
    out_t = nc.dram_tensor("out_t", [D, TOWN], F32, kind="ExternalOutput").ap()

    x1t = nc.dram_tensor("x1t", [D, S], F32, kind="Internal").ap()
    g8 = nc.dram_tensor("g8", [NO, NCH, 128, 128], FP8, kind="Internal").ap()
    u8 = nc.dram_tensor("u8", [NO, NCH, 128, 128], FP8, kind="Internal").ap()
    d8 = nc.dram_tensor("d8", [NCH, NO, 128, 128], FP8, kind="Internal").ap()
    wqb = nc.dram_tensor("wqb", [H, NCH, 128, 128], BF16, kind="Internal").ap()
    wkb = nc.dram_tensor("wkb", [H, NCH, 128, 128], BF16, kind="Internal").ap()
    wvb = nc.dram_tensor("wvb", [D, D], BF16, kind="Internal").ap()
    wob = nc.dram_tensor("wob", [NCH, H, 128, 128], BF16, kind="Internal").ap()
    qt_hi = nc.dram_tensor("qt_hi", [H, 128, TOWN], BF16, kind="Internal").ap()
    qt_lo = nc.dram_tensor("qt_lo", [H, 128, TOWN], BF16, kind="Internal").ap()
    kt_hi = nc.dram_tensor("kt_hi", [H, 128, S], BF16, kind="Internal").ap()
    kt_lo = nc.dram_tensor("kt_lo", [H, 128, S], BF16, kind="Internal").ap()
    v_hi = nc.dram_tensor("v_hi", [H, 128, S // 128, 128], BF16, kind="Internal").ap()
    v_lo = nc.dram_tensor("v_lo", [H, 128, S // 128, 128], BF16, kind="Internal").ap()

    with ExitStack() as st:
        E = st.enter_context
        X2 = E(nc.sbuf_tensor("X2", [128, NCH * TB], F32))
        SCR = E(nc.sbuf_tensor("SCR", [128, 6 * 512], F32))
        BVB = E(nc.sbuf_tensor("BVB", [128, D], F32))
        MRB = E(nc.sbuf_tensor("MRB", [128, 1024], F32))
        CST = E(nc.sbuf_tensor("CST", [128, C_TOT], F32))
        MCOL = E(nc.sbuf_tensor("MCOL", [128, H], F32))
        ONESB = E(nc.sbuf_tensor("ONESB", [128, 1], BF16))
        ROW1 = E(nc.sbuf_tensor("ROW1", [1, 128], F32))
        ROWS = E(nc.sbuf_tensor("ROWS", [1, 2048], F32))
        RSUM = E(nc.sbuf_tensor("RSUM", [1, 512], F32))
        QN = E(nc.sbuf_tensor("QN", [1, H], F32))
        KN = E(nc.sbuf_tensor("KN", [1, H], F32))
        M16 = E(nc.sbuf_tensor("M16", [1, H], F32))
        TMP1 = E(nc.sbuf_tensor("TMP1", [1, 16], F32))
        A1 = E(nc.sbuf_tensor("A1", [128, 8192], BF16))
        A2 = E(nc.sbuf_tensor("A2", [128, 8192], BF16))
        A3 = E(nc.sbuf_tensor("A3", [128, 6144], BF16))
        A4 = E(nc.sbuf_tensor("A4", [128, 6144], BF16))
        A5 = E(nc.sbuf_tensor("A5", [128, 8192], BF16))
        A6 = E(nc.sbuf_tensor("A6", [128, 6144], BF16))
        WI = E(nc.sbuf_tensor("WI", [128, 4096], I32))
        W8A = E(nc.sbuf_tensor("W8A", [128, 4096], FP8))
        W8B = E(nc.sbuf_tensor("W8B", [128, 8192], FP8))

        pA = E(nc.psum_tensor("pA", [128, 512], F32))
        pB = E(nc.psum_tensor("pB", [128, 512], F32))
        pC = E(nc.psum_tensor("pC", [128, 512], F32))
        pD = E(nc.psum_tensor("pD", [128, 512], F32))
        pE = E(nc.psum_tensor("pE", [128, 512], F32))
        pM = E(nc.psum_tensor("pM", [128, 512], F32))
        pR = E(nc.psum_tensor("pR", [1, 512], F32))
        pN = E(nc.psum_tensor("pN", [1, 512], F32))

        s_ld = E(nc.semaphore("s_ld"))
        s_g = E(nc.semaphore("s_g"))
        s_sc = E(nc.semaphore("s_sc"))
        s_t = E(nc.semaphore("s_t"))
        s_v = E(nc.semaphore("s_v"))
        s_s = E(nc.semaphore("s_s"))
        block = E(nc.Block())

        p = P()
        for nm, h in [("ld", s_ld), ("g", s_g), ("sc", s_sc),
                      ("t", s_t), ("v", s_v), ("s", s_s)]:
            p.add(nm, h)

        def body(engname, fn):
            getattr(block, engname)(fn)

        def _(e):
            for o_, i_ in [
                (CST[:, C_BQ:C_BQ + NCH], ap["bq2"]),
                (CST[:, C_BK:C_BK + NCH], ap["bk2"]),
                (CST[:, C_BO:C_BO + NCH], ap["bo2"]),
                (CST[:, C_GS:C_GS + NO], ap["gs2"]),
                (CST[:, C_US:C_US + NO], ap["us2"]),
                (CST[:, C_DSH:C_DSH + NCH], ap["ds2"]),
                (ROWS[0:1, 0:D], ap["bv_row"]),
            ]:
                p.inc(e.dma_start(out=o_, in_=i_), "ld", 16)
        body("sync", _)
        ld_consts = p.cnt("ld")

        def _(e):
            e.memset(CST[:, C_ONES:C_ONES + 1], 1.0)
            e.memset(ONESB[:, 0:1], 1.0)
            e.memset(ROW1[0:1, 0:128], 1.0)
            p.wait(e, "ld", ld_consts)
            ins = e.tensor_scalar(CST[:, C_DSH:C_DSH + NCH],
                                  CST[:, C_DSH:C_DSH + NCH], 0.5, None, OP.mult)
            p.inc(ins, "v")
        body("vector", _)
        v_consts = p.cnt("v")

        for fb in range(4):
            bank = pM if fb % 2 == 0 else pE
            bname = "pM" if fb % 2 == 0 else "pE"

            def _(e, fb=fb, bank=bank, bname=bname):
                if fb == 0:
                    p.wait(e, "v", v_consts)
                p.guard(e, bname)
                ins = e.matmul(bank[:, 0:512], ROW1[0:1, 0:128],
                               ROWS[0:1, fb * 512:(fb + 1) * 512],
                               start=True, stop=True)
                p.inc(ins, "t")
            body("tensor", _)
            tv = p.cnt("t")

            def _(e, fb=fb, bank=bank, tv=tv):
                p.wait(e, "t", tv)
                ins = e.tensor_copy(BVB[:, fb * 512:(fb + 1) * 512], bank[:, 0:512])
                p.inc(ins, "v")
            body("vector", _)
            p.rel(bname, "v")

        def prep_q(src_ap_2048, dst_store_fn, qi, store_q=None):
            par = qi % 4
            xq = X2[:, par * 2048:(par + 1) * 2048]

            def _(e, src_ap_2048=src_ap_2048, par=par, xq=xq):
                p.guard(e, f"X2q{par}")
                ins = e.dma_start(out=xq, in_=src_ap_2048)
                p.inc(ins, "ld", 16)
            body("sync", _)
            ldv = p.cnt("ld")
            eng = "vector" if qi % 2 == 0 else "scalar"
            sem = "v" if qi % 2 == 0 else "s"
            half = qi % 2
            rot = (qi // 2) % 2
            f8a = W8A[:, half * 2048:(half + 1) * 2048]
            wi = WI[:, half * 2048:(half + 1) * 2048]
            f8b = W8B[:, (half * 2 + rot) * 2048:(half * 2 + rot + 1) * 2048]
            wkey = f"W8{eng[0]}{rot}"

            def _(e, ldv=ldv, xq=xq, f8a=f8a, wi=wi, f8b=f8b, eng=eng, sem=sem,
                  wkey=wkey):
                p.wait(e, "ld", ldv)
                p.guard(e, wkey)
                if eng == "vector":
                    e.tensor_copy(f8a, xq)
                    e.tensor_scalar(wi, f8a, 512.0, None, OP.mult)
                    ins = e.tensor_scalar(f8b, wi, 1.0 / 512.0, None, OP.mult)
                else:
                    e.copy(f8a, xq)
                    e.mul(wi, f8a, 512.0)
                    ins = e.mul(f8b, wi, 1.0 / 512.0)
                p.inc(ins, sem)
            body(eng, _)
            cv = p.cnt(sem)
            p.free[f"X2q{par}"] = (sem, cv)

            def _(e, cv=cv, sem=sem, f8b=f8b):
                p.wait(e, sem, cv)
                ins = dst_store_fn(e, f8b)
                p.inc(ins, "sc" if store_q == "scalar" else "g", 16)
            body("gpsimd" if store_q != "scalar" else "scalar", _)
            p.free[wkey] = (("sc" if store_q == "scalar" else "g"), p.cnt(
                "sc" if store_q == "scalar" else "g"))

        qi = 0
        for srcw, dstw in ((ap["gate_w"], g8), (ap["up_w"], u8)):
            for c in range(NCH):
                for sub in range(3):
                    o0 = sub * 16

                    def store(e, f8b, dstw=dstw, c=c, o0=o0):
                        return e.dma_start(
                            out=dstw[o0:o0 + 16, c].rearrange("o p w -> p o w"),
                            in_=f8b.rearrange("p (o w) -> p o w", w=128))
                    prep_q(srcw[c * 128:(c + 1) * 128,
                                sub * 2048:(sub + 1) * 2048], store, qi)
                    qi += 1
        g_after_gu8 = p.cnt("g")
        for o in range(NO):
            def store(e, f8b, o=o):
                return e.dma_start(
                    out=d8[:, o].rearrange("d p w -> p d w"),
                    in_=f8b.rearrange("p (d w) -> p d w", w=128))
            prep_q(ap["down_w"][o * 128:(o + 1) * 128, :], store, qi,
                   store_q="scalar")
            qi += 1
        dn_ready = ("sc", p.cnt("sc"))

        def prep_attn(codes, sx, store_fn):
            for c in range(NCH):
                par = c % 2
                xc = X2[:, par * 4096:par * 4096 + 2048]
                xs = X2[:, par * 4096 + 2048:par * 4096 + 4096]
                dq = A1[:, par * 2048:(par + 1) * 2048]

                def _(e, c=c, xc=xc, xs=xs, par=par):
                    p.guard(e, f"X2h{par}")
                    p.guard(e, f"X2q{2 * par}")
                    p.guard(e, f"X2q{2 * par + 1}")
                    p.inc(e.dma_start(out=xc,
                                      in_=codes[c * 128:(c + 1) * 128, :]), "ld", 16)
                    p.inc(e.dma_start(out=xs,
                                      in_=sx[c * 128:(c + 1) * 128, :]), "ld", 16)
                body("sync", _)
                ldv = p.cnt("ld")

                def _(e, c=c, ldv=ldv, xc=xc, xs=xs, dq=dq, par=par):
                    p.wait(e, "ld", ldv)
                    p.guard(e, f"A1h{par}")
                    ins = e.tensor_tensor(dq, xc, xs, OP.mult)
                    p.inc(ins, "v")
                body("vector", _)
                vv = p.cnt("v")
                p.free[f"X2h{par}"] = ("v", vv)

                def _(e, c=c, vv=vv, dq=dq):
                    p.wait(e, "v", vv)
                    ins = store_fn(e, c, dq)
                    p.inc(ins, "sc", 16)
                body("scalar", _)
                p.free[f"A1h{par}"] = ("sc", p.cnt("sc"))

        prep_attn(ap["wq_codes"], ap["wq_sx"],
                  lambda e, c, dq: e.dma_start(
                      out=wqb[:, c].rearrange("f p w -> p f w"),
                      in_=dq.rearrange("p (f w) -> p f w", w=128)))
        prep_attn(ap["wk_codes"], ap["wk_sx"],
                  lambda e, c, dq: e.dma_start(
                      out=wkb[:, c].rearrange("f p w -> p f w"),
                      in_=dq.rearrange("p (f w) -> p f w", w=128)))
        prep_attn(ap["wv_codes"], ap["wv_sx"],
                  lambda e, c, dq: e.dma_start(
                      out=wvb[c * 128:(c + 1) * 128, :], in_=dq))
        prep_attn(ap["wo_codes"], ap["wo_sx"],
                  lambda e, c, dq: e.dma_start(
                      out=wob[:, c].rearrange("d p w -> p d w"),
                      in_=dq.rearrange("p (d w) -> p d w", w=128)))
        p.rel("A1", "sc")
        p.rel("X2", "v")
        g_after_p0 = p.cnt("g")
        sc_after_p0 = p.cnt("sc")

        def dummy_out(x_sem):
            for tcx in range(4):
                def _(e, tcx=tcx, x_sem=x_sem):
                    p.wait(e, x_sem[0], x_sem[1])
                    ins = e.dma_start(
                        out=out_t[:, tcx * TB:(tcx + 1) * TB]
                        .rearrange("(c p) t -> p c t", p=128),
                        in_=X2[:, :].rearrange("p (c t) -> p c t", t=TB))
                    p.inc(ins, "g", 16)
                body("gpsimd", _)
            gf = p.cnt("g")

            def _(e, gf=gf):
                p.wait(e, "g", gf)
            body("gpsimd", _)

        if stop == 'p0':
            dummy_out(("g", g_after_p0))
            return nc

        def emit_ln(x_ready):
            for c in range(NCH):
                slot = SCR[:, (c % 2) * 512:(c % 2) * 512 + 512]

                def _(e, c=c, slot=slot):
                    if c == 0:
                        p.wait(e, x_ready[0], x_ready[1])
                    p.guard(e, f"SCR{c % 2}")
                    ins = e.activation(slot, X2[:, c * TB:(c + 1) * TB], AF.Square)
                    p.inc(ins, "s")
                body("scalar", _)
                sv = p.cnt("s")

                def _(e, c=c, slot=slot, sv=sv, x_ready=x_ready):
                    if c == 0:
                        p.wait(e, x_ready[0], x_ready[1])
                        p.guard(e, "pR")
                        p.guard(e, "pN")
                    p.wait(e, "s", sv)
                    e.matmul(pR[0:1, :], CST[:, C_ONES:C_ONES + 1],
                             X2[:, c * TB:(c + 1) * TB],
                             start=(c == 0), stop=(c == NCH - 1))
                    ins = e.matmul(pN[0:1, :], CST[:, C_ONES:C_ONES + 1], slot,
                                   start=(c == 0), stop=(c == NCH - 1))
                    p.inc(ins, "t")
                body("tensor", _)
                p.free[f"SCR{c % 2}"] = ("t", p.cnt("t"))
            t_stats = p.cnt("t")

            def _(e):
                p.wait(e, "t", t_stats)
                e.tensor_scalar(ROWS[0:1, 0:512], pR[0:1, :], 1.0 / D, None, OP.mult)
                e.tensor_scalar(ROWS[0:1, 512:1024], pN[0:1, :], 1.0 / D, None, OP.mult)
                e.tensor_tensor(ROWS[0:1, 1536:2048], ROWS[0:1, 0:512],
                                ROWS[0:1, 0:512], OP.mult)
                e.tensor_tensor(ROWS[0:1, 1024:1536], ROWS[0:1, 512:1024],
                                ROWS[0:1, 1536:2048], OP.subtract)
                ins = e.tensor_scalar(ROWS[0:1, 1024:1536], ROWS[0:1, 1024:1536],
                                      EPS, None, OP.add)
                p.inc(ins, "v")
            body("vector", _)
            p.rel("pR", "v")
            p.rel("pN", "v")
            vv = p.cnt("v")

            def _(e, vv=vv):
                p.wait(e, "v", vv)
                ins = e.activation(ROWS[0:1, 512:1024], ROWS[0:1, 1024:1536],
                                   AF.Sqrt)
                p.inc(ins, "s")
            body("scalar", _)
            sv = p.cnt("s")

            def _(e, sv=sv):
                p.wait(e, "s", sv)
                e.reciprocal(ROWS[0:1, 1536:2048], ROWS[0:1, 512:1024])
                e.tensor_tensor(RSUM[0:1, :], ROWS[0:1, 1536:2048],
                                ROWS[0:1, 1536:2048], OP.mult)
                e.tensor_tensor(RSUM[0:1, :], RSUM[0:1, :],
                                ROWS[0:1, 1024:1536], OP.mult)
                e.tensor_scalar(RSUM[0:1, :], RSUM[0:1, :], -0.5, 1.5,
                                OP.mult, OP.add)
                ins = e.tensor_tensor(ROWS[0:1, 512:1024], ROWS[0:1, 1536:2048],
                                      RSUM[0:1, :], OP.mult)
                p.inc(ins, "v")
            body("vector", _)
            vv = p.cnt("v")

            def _(e, vv=vv):
                p.wait(e, "v", vv)
                p.guard(e, "pM")
                p.guard(e, "pE")
                e.matmul(pM[:, :], ROW1[0:1, :], ROWS[0:1, 0:512],
                         start=True, stop=True)
                ins = e.matmul(pE[:, :], ROW1[0:1, :], ROWS[0:1, 512:1024],
                               start=True, stop=True)
                p.inc(ins, "t")
            body("tensor", _)
            tv = p.cnt("t")

            def _(e, tv=tv):
                p.wait(e, "t", tv)
                e.tensor_copy(MRB[:, 0:512], pM[:, :])
                ins = e.tensor_copy(MRB[:, 512:1024], pE[:, :])
                p.inc(ins, "v")
                p.guard(e, "A1")
                p.guard(e, "A2")
                for c in range(NCH):
                    e.tensor_tensor(SCR[:, 0:512], X2[:, c * TB:(c + 1) * TB],
                                    MRB[:, 0:512], OP.subtract)
                    e.tensor_tensor(SCR[:, 512:1024], SCR[:, 0:512],
                                    MRB[:, 512:1024], OP.mult)
                    e.tensor_copy(A1[:, c * TB:(c + 1) * TB], SCR[:, 512:1024])
                    ins = e.tensor_tensor(A2[:, c * TB:(c + 1) * TB],
                                          SCR[:, 512:1024],
                                          A1[:, c * TB:(c + 1) * TB], OP.subtract)
                p.inc(ins, "v")
            body("vector", _)
            p.rel("pM", "v")
            p.rel("pE", "v")
            return p.cnt("v")

        def emit_ffn(x_ready, w_ready_g, w_ready_dn=None):
            if w_ready_dn is None:
                w_ready_dn = ("g", w_ready_g)
            v_yt = emit_ln(x_ready)
            if ffn_mode == 'ln':
                return v_yt
            gbanks = [(pA, "pA"), (pC, "pC")]
            ubanks = [(pB, "pB"), (pD, "pD")]
            for q in range(4):
                for oi in range(OQ):
                    o = q * OQ + oi
                    if o >= olim:
                        continue
                    par = o % 2
                    gslot = A5[:, par * 4096:par * 4096 + 2048]
                    uslot = A5[:, par * 4096 + 2048:par * 4096 + 4096]

                    def _(e, o=o, par=par, gslot=gslot, uslot=uslot,
                          w_ready_g=w_ready_g):
                        if o == 0:
                            p.wait(e, "g", w_ready_g)
                        for k_ in (("A5q0", "A5q1") if par == 0 else ("A5q2", "A5q3")):
                            p.guard(e, k_)
                        p.inc(e.dma_start(
                            out=gslot.rearrange("p (c w) -> p c w", w=128),
                            in_=g8[o].rearrange("c p w -> p c w")), "g", 16)
                        p.inc(e.dma_start(
                            out=uslot.rearrange("p (c w) -> p c w", w=128),
                            in_=u8[o].rearrange("c p w -> p c w")), "g", 16)
                    body("gpsimd", _)
                    gv = p.cnt("g")

                    def _(e, o=o, par=par, gslot=gslot, uslot=uslot, gv=gv,
                          v_yt=v_yt):
                        p.wait(e, "g", gv)
                        if o == 0:
                            p.wait(e, "v", v_yt)
                        gp, gn = gbanks[par]
                        up, un = ubanks[par]
                        p.guard(e, gn)
                        p.guard(e, un)
                        for c in range(NCH):
                            lt = gslot[:, c * 128:(c + 1) * 128]
                            e.matmul(gp[:, :], lt, A1[:, c * TB:(c + 1) * TB],
                                     start=(c == 0), stop=False)
                            ins = e.matmul(gp[:, :], lt, A2[:, c * TB:(c + 1) * TB],
                                           start=False, stop=(c == NCH - 1))
                        p.inc(ins, "t")
                        for c in range(NCH):
                            lt = uslot[:, c * 128:(c + 1) * 128]
                            e.matmul(up[:, :], lt, A1[:, c * TB:(c + 1) * TB],
                                     start=(c == 0), stop=False)
                            ins = e.matmul(up[:, :], lt, A2[:, c * TB:(c + 1) * TB],
                                           start=False, stop=(c == NCH - 1))
                        p.inc(ins, "t")
                    body("tensor", _)
                    for k_ in (("A5q0", "A5q1") if par == 0 else ("A5q2", "A5q3")):
                        p.free[k_] = ("t", p.cnt("t"))
                    tv = p.cnt("t")
                    if skip_sv >= 2:
                        p.free[gbanks[par][1]] = ("t", p.cnt("t"))
                        p.free[ubanks[par][1]] = ("t", p.cnt("t"))
                        continue

                    def _(e, o=o, par=par, tv=tv):
                        p.wait(e, "t", tv)
                        p.guard(e, f"SCR{par}")
                        gp, _gn = gbanks[par]
                        sil = SCR[:, par * 512:par * 512 + 512]
                        ins = e.activation(sil, gp[:, :], AF.Silu,
                                           scale=CST[:, C_GS + o:C_GS + o + 1])
                        p.inc(ins, "s")
                    body("scalar", _)
                    p.free[gbanks[par][1]] = ("s", p.cnt("s"))
                    sv = p.cnt("s")
                    if skip_sv >= 1:
                        continue

                    def _(e, o=o, oi=oi, par=par, sv=sv):
                        p.wait(e, "s", sv)
                        p.guard(e, f"SCR{2 + par}")
                        if oi == 0:
                            p.guard(e, "SCR4")
                            for k_ in ("A3", "A4w0", "A4w1", "A4r"):
                                p.guard(e, k_)
                        up, _un = ubanks[par]
                        sil = SCR[:, par * 512:par * 512 + 512]
                        upz = SCR[:, 1024 + par * 512:1024 + par * 512 + 512]
                        ff = SCR[:, 2048:2560]
                        e.tensor_scalar(upz, up[:, :],
                                        CST[:, C_US + o:C_US + o + 1], None,
                                        OP.mult)
                        e.tensor_tensor(ff, sil, upz, OP.mult)
                        e.tensor_copy(A3[:, oi * 512:(oi + 1) * 512], ff)
                        ins = e.tensor_tensor(A4[:, oi * 512:(oi + 1) * 512],
                                              ff, A3[:, oi * 512:(oi + 1) * 512],
                                              OP.subtract)
                        p.inc(ins, "v")
                    body("vector", _)
                    p.free[ubanks[par][1]] = ("v", p.cnt("v"))
                    p.free[f"SCR{par}"] = ("v", p.cnt("v"))
                    p.free[f"SCR{2 + par}"] = ("v", p.cnt("v"))
                v_fuse = p.cnt("v")
                if ffn_mode == 'gu':
                    for k_ in ("A3", "A4w0", "A4w1", "A4r"):
                        p.free[k_] = ("v", p.cnt("v"))
                    continue

                dbanks = [(pA, "pA"), (pB, "pB")]
                for dg in range(8):
                    par = dg % 2
                    dwslot = A6[:, par * 3072:par * 3072 + 3072]

                    def _(e, q=q, dg=dg, par=par, dwslot=dwslot,
                          w_ready_dn=w_ready_dn):
                        if q == 0 and dg == 0:
                            p.wait(e, w_ready_dn[0], w_ready_dn[1])
                        for j_ in range(6):
                            p.guard(e, f"A6_{par * 6 + j_}")
                        for dd in range(2):
                            p.inc(e.dma_start(
                                out=dwslot[:, dd * 1536:(dd + 1) * 1536]
                                .rearrange("p (o w) -> p o w", w=128),
                                in_=d8[dg * 2 + dd, q * OQ:(q + 1) * OQ]
                                .rearrange("o p w -> p o w")), "g", 16)
                    body("gpsimd", _)
                    gv = p.cnt("g")

                    for dd in range(2):
                        d = dg * 2 + dd
                        bank, bname = dbanks[d % 2]

                        def _(e, q=q, d=d, dd=dd, par=par, dwslot=dwslot,
                              gv=gv, bank=bank, bname=bname, v_fuse=v_fuse):
                            if dd == 0:
                                p.wait(e, "g", gv)
                                if d == 0:
                                    p.wait(e, "v", v_fuse)
                            p.guard(e, bname)
                            for oi in range(OQ):
                                lt = dwslot[:, dd * 1536 + oi * 128:
                                            dd * 1536 + (oi + 1) * 128]
                                e.matmul(bank[:, :], lt,
                                         A3[:, oi * 512:(oi + 1) * 512],
                                         start=(oi == 0), stop=False)
                                ins = e.matmul(bank[:, :], lt,
                                               A4[:, oi * 512:(oi + 1) * 512],
                                               start=False, stop=(oi == OQ - 1))
                            p.inc(ins, "t")
                        body("tensor", _)
                        tv = p.cnt("t")

                        def _(e, d=d, bank=bank, tv=tv):
                            p.wait(e, "t", tv)
                            ins = e.scalar_tensor_tensor(
                                X2[:, d * TB:(d + 1) * TB], bank[:, :],
                                CST[:, C_DSH + d:C_DSH + d + 1],
                                X2[:, d * TB:(d + 1) * TB], OP.mult, OP.add)
                            p.inc(ins, "v")
                        body("vector", _)
                        p.free[bname] = ("v", p.cnt("v"))
                    for j_ in range(6):
                        p.free[f"A6_{par * 6 + j_}"] = ("t", p.cnt("t"))
                for k_ in ("A3", "A4w0", "A4w1", "A4r"):
                    p.free[k_] = ("t", p.cnt("t"))
            p.rel("A1", "v")
            p.rel("A2", "v")
            return p.cnt("v")

        x1_store_g = []
        for tb in range(p1_tbs):
            def _(e, tb=tb):
                p.guard(e, "X2")
                ins = e.dma_start(
                    out=X2[:, :].rearrange("p (c t) -> p c t", t=TB),
                    in_=ap["x0t"][:, tb * TB:(tb + 1) * TB]
                    .rearrange("(c p) t -> p c t", p=128))
                p.inc(ins, "ld", 16)
            body("sync", _)
            xv = p.cnt("ld")
            v_done = emit_ffn(("ld", xv), g_after_gu8, dn_ready)

            def _(e, tb=tb, v_done=v_done):
                p.wait(e, "v", v_done)
                ins = e.dma_start(
                    out=x1t[:, tb * TB:(tb + 1) * TB]
                    .rearrange("(c p) t -> p c t", p=128),
                    in_=X2[:, :].rearrange("p (c t) -> p c t", t=TB))
                p.inc(ins, "g", 16)
            body("gpsimd", _)
            x1_store_g.append(p.cnt("g"))
            p.rel("X2", "g")

        if stop == 'p1':
            dummy_out(("g", p.cnt("g")))
            return nc

        kq_banks = [(pA, "pA"), (pB, "pB")]
        for tc in range(p2_tcs):
            def _(e, tc=tc):
                p.guard(e, "X2")
                p.wait(e, "g", x1_store_g[min(tc, len(x1_store_g) - 1)])
                ins = e.dma_start(
                    out=X2[:, :].rearrange("p (c t) -> p c t", t=TB),
                    in_=x1t[:, tc * TB:(tc + 1) * TB]
                    .rearrange("(c p) t -> p c t", p=128))
                p.inc(ins, "ld", 16)
            body("sync", _)
            xv = p.cnt("ld")
            v_yt = emit_ln(("ld", xv))

            projs = [("k", wkb, C_BK, kt_hi, kt_lo, KN, tc * TB)]
            if tc < 4:
                projs.append(("q", wqb, C_BQ, qt_hi, qt_lo, QN, tc * TB))
            for pname, wtile, cbias, dhi, dlo, nmax, coff in projs:
                for f in range(H):
                    par = f % 2
                    wpar = f % 4
                    wslot = A5[:, wpar * 2048:wpar * 2048 + 2048]

                    def _(e, f=f, wpar=wpar, wslot=wslot, wtile=wtile):
                        p.guard(e, f"A5q{wpar}")
                        p.wait(e, "sc", sc_after_p0)
                        ins = e.dma_start(
                            out=wslot.rearrange("p (c w) -> p c w", w=128),
                            in_=wtile[f].rearrange("c p w -> p c w"))
                        p.inc(ins, "ld", 16)
                    body("sync", _)
                    ldv = p.cnt("ld")
                    bank, bname = kq_banks[par]

                    def _(e, f=f, par=par, wslot=wslot, ldv=ldv, bank=bank,
                          bname=bname, v_yt=v_yt):
                        p.wait(e, "ld", ldv)
                        if f == 0:
                            p.wait(e, "v", v_yt)
                        p.guard(e, bname)
                        for c in range(NCH):
                            lt = wslot[:, c * 128:(c + 1) * 128]
                            e.matmul(bank[:, :], lt, A1[:, c * TB:(c + 1) * TB],
                                     start=(c == 0), stop=False)
                            ins = e.matmul(bank[:, :], lt, A2[:, c * TB:(c + 1) * TB],
                                           start=False, stop=(c == NCH - 1))
                        p.inc(ins, "t")
                    body("tensor", _)
                    p.free[f"A5q{wpar}"] = ("t", p.cnt("t"))
                    tv = p.cnt("t")

                    kslot = 1024 + par * 1024

                    def _(e, f=f, par=par, bank=bank, tv=tv, cbias=cbias,
                          kslot=kslot):
                        p.wait(e, "t", tv)
                        p.guard(e, f"SCR{par}")
                        kf = SCR[:, 0:512] if par == 0 else SCR[:, 512:1024]
                        ins = e.activation(kf, bank[:, :], AF.Identity,
                                           bias=CST[:, cbias + f:cbias + f + 1])
                        p.inc(ins, "s")
                    body("scalar", _)
                    p.free[bname] = ("s", p.cnt("s"))
                    sv = p.cnt("s")

                    def _(e, f=f, par=par, sv=sv, kslot=kslot):
                        p.wait(e, "s", sv)
                        p.guard(e, f"A6_{par}")
                        kf = SCR[:, 0:512] if par == 0 else SCR[:, 512:1024]
                        ins = e.tensor_tensor(A6[:, par * 512:par * 512 + 512],
                                              kf, kf, OP.mult)
                        p.inc(ins, "v")
                        p.guard(e, f"A6_{2 + 2 * par}")
                        p.guard(e, f"A6_{3 + 2 * par}")
                        e.tensor_copy(A6[:, kslot:kslot + 512], kf)
                        ins = e.tensor_tensor(A6[:, kslot + 512:kslot + 1024],
                                              kf, A6[:, kslot:kslot + 512],
                                              OP.subtract)
                        p.inc(ins, "v")
                    body("vector", _)
                    p.free[f"SCR{par}"] = ("v", p.cnt("v"))
                    vv = p.cnt("v")

                    nb, nbk = (pN, "pN") if f % 2 == 0 else (pR, "pR")

                    def _(e, f=f, par=par, vv=vv, nb=nb, nbk=nbk):
                        p.wait(e, "v", vv - 1)
                        p.guard(e, nbk)
                        ins = e.matmul(nb[0:1, :], ONESB[:, 0:1],
                                       A6[:, par * 512:par * 512 + 512],
                                       start=True, stop=True)
                        p.inc(ins, "t")
                    body("tensor", _)
                    p.free[f"A6_{par}"] = ("t", p.cnt("t"))
                    tnv = p.cnt("t")

                    def _(e, f=f, tc=tc, tnv=tnv, nmax=nmax, nb=nb):
                        p.wait(e, "t", tnv)
                        if tc == 0:
                            ins = e.tensor_reduce(nmax[0:1, f:f + 1], nb[0:1, :],
                                                  AX, OP.max)
                        else:
                            e.tensor_reduce(TMP1[0:1, 0:1], nb[0:1, :], AX, OP.max)
                            ins = e.tensor_tensor(nmax[0:1, f:f + 1],
                                                  nmax[0:1, f:f + 1],
                                                  TMP1[0:1, 0:1], OP.max)
                        p.inc(ins, "v")
                    body("vector", _)
                    p.free[nbk] = ("v", p.cnt("v"))

                    def _(e, f=f, par=par, vv=vv, dhi=dhi, dlo=dlo, coff=coff,
                          kslot=kslot):
                        p.wait(e, "v", vv)
                        p.inc(e.dma_start(out=dhi[f][:, coff:coff + 512],
                                          in_=A6[:, kslot:kslot + 512]), "g", 16)
                        p.inc(e.dma_start(out=dlo[f][:, coff:coff + 512],
                                          in_=A6[:, kslot + 512:kslot + 1024]),
                              "g", 16)
                    body("gpsimd", _)
                    p.free[f"A6_{2 + 2 * par}"] = ("g", p.cnt("g"))
                    p.free[f"A6_{3 + 2 * par}"] = ("g", p.cnt("g"))

            v_banks = [(pC, "pC"), (pD, "pD")]
            for fb in range(4):
                for half in range(2):
                    def _(e, tc=tc, fb=fb, half=half):
                        p.guard(e, f"A5q{half * 2}")
                        p.guard(e, f"A5q{half * 2 + 1}")
                        p.wait(e, "sc", sc_after_p0)
                        ins = e.dma_start(
                            out=A5[:, half * 4096:(half + 1) * 4096]
                            .rearrange("p (c w) -> p c w", w=512),
                            in_=wvb[half * 1024:(half + 1) * 1024,
                                    fb * 512:(fb + 1) * 512]
                            .rearrange("(c p) w -> p c w", p=128))
                        p.inc(ins, "ld", 16)
                    body("sync", _)
                ldv = p.cnt("ld")
                for tt in range(4):
                    bank, bname = v_banks[tt % 2]

                    def _(e, fb=fb, tt=tt, ldv=ldv, bank=bank, bname=bname,
                          v_yt=v_yt):
                        p.wait(e, "ld", ldv)
                        if fb == 0 and tt == 0:
                            p.wait(e, "v", v_yt)
                        p.guard(e, bname)
                        for c in range(NCH):
                            half = c // 8
                            wsl = A5[:, half * 4096 + (c % 8) * 512:
                                     half * 4096 + (c % 8) * 512 + 512]
                            lt1 = A1[:, c * TB + tt * 128:c * TB + (tt + 1) * 128]
                            lt2 = A2[:, c * TB + tt * 128:c * TB + (tt + 1) * 128]
                            e.matmul(bank[:, :], lt1, wsl,
                                     start=(c == 0), stop=False)
                            ins = e.matmul(bank[:, :], lt2, wsl,
                                           start=False, stop=(c == NCH - 1))
                        p.inc(ins, "t")
                    body("tensor", _)
                    if tt == 3:
                        for k_ in ("A5q0", "A5q1", "A5q2", "A5q3"):
                            p.free[k_] = ("t", p.cnt("t"))
                    tv = p.cnt("t")
                    vslot = 2048 + (tt % 2) * 2048

                    def _(e, fb=fb, tt=tt, tv=tv, bank=bank, vslot=vslot):
                        p.wait(e, "t", tv)
                        p.guard(e, f"A6_{4 + 4 * (tt % 2)}")
                        p.guard(e, f"A6_{5 + 4 * (tt % 2)}")
                        vf = SCR[:, 2048:2560]
                        e.tensor_tensor(vf, bank[:, :],
                                        BVB[:, fb * 512:(fb + 1) * 512], OP.add)
                        e.tensor_copy(A6[:, vslot:vslot + 512], vf)
                        ins = e.tensor_tensor(A6[:, vslot + 512:vslot + 1024],
                                              vf, A6[:, vslot:vslot + 512],
                                              OP.subtract)
                        p.inc(ins, "v")
                    body("vector", _)
                    p.free[bname] = ("v", p.cnt("v"))
                    vv = p.cnt("v")

                    def _(e, tc=tc, fb=fb, tt=tt, vv=vv, vslot=vslot):
                        p.wait(e, "v", vv)
                        tile = tc * 4 + tt
                        for j in range(4):
                            h = fb * 4 + j
                            p.inc(e.dma_start(
                                out=v_hi[h][:, tile, :],
                                in_=A6[:, vslot + j * 128:vslot + (j + 1) * 128]),
                                "g", 16)
                    body("gpsimd", _)
                    p.free[f"A6_{4 + 4 * (tt % 2)}"] = ("g", p.cnt("g"))

                    def _(e, tc=tc, fb=fb, tt=tt, vv=vv, vslot=vslot):
                        p.wait(e, "v", vv)
                        tile = tc * 4 + tt
                        for j in range(4):
                            h = fb * 4 + j
                            p.inc(e.dma_start(
                                out=v_lo[h][:, tile, :],
                                in_=A6[:, vslot + 512 + j * 128:
                                       vslot + 512 + (j + 1) * 128]), "sc", 16)
                    body("scalar", _)
                    p.free[f"A6_{5 + 4 * (tt % 2)}"] = ("sc", p.cnt("sc"))
            p.rel("X2", "v")
            p.rel("A1", "t")
            p.rel("A2", "t")
        g_after_p2 = p.cnt("g")
        sc_after_p2 = p.cnt("sc")
        ld_after_p2 = p.cnt("ld")

        if stop == 'p2':
            dummy_out(("sc", p.cnt("sc")))
            return nc

        def _(e):
            p.wait(e, "v", p.cnt("v"))
            ins = e.tensor_tensor(TMP1[0:1, 0:H], QN[0:1, :], KN[0:1, :], OP.mult)
            p.inc(ins, "v")
        body("vector", _)
        vv = p.cnt("v")

        def _(e, vv=vv):
            p.wait(e, "v", vv)
            ins = e.activation(M16[0:1, :], TMP1[0:1, 0:H], AF.Sqrt)
            p.inc(ins, "s")
        body("scalar", _)
        sv = p.cnt("s")

        def _(e, sv=sv):
            p.wait(e, "s", sv)
            ins = e.tensor_scalar(M16[0:1, :], M16[0:1, :],
                                  -1.01 * SCALE, None, OP.mult)
            p.inc(ins, "v")
        body("vector", _)
        vv = p.cnt("v")
        for hh in range(H):
            def _(e, hh=hh, vv=vv):
                if hh == 0:
                    p.wait(e, "v", vv)
                p.guard(e, "pM")
                ins = e.matmul(pM[:, 0:1], ROW1[0:1, :], M16[0:1, hh:hh + 1],
                               start=True, stop=True)
                p.inc(ins, "t")
            body("tensor", _)
            tv = p.cnt("t")

            def _(e, hh=hh, tv=tv):
                p.wait(e, "t", tv)
                ins = e.tensor_copy(MCOL[:, hh:hh + 1], pM[:, 0:1])
                p.inc(ins, "v")
            body("vector", _)
            p.rel("pM", "v")

        st_banks = [(pA, "pA"), (pB, "pB")]
        for tc in range(f_tcs):
            def _(e, tc=tc):
                p.guard(e, "X2")
                p.wait(e, "g", x1_store_g[min(tc, len(x1_store_g) - 1)])
                ins = e.dma_start(
                    out=X2[:, :].rearrange("p (c t) -> p c t", t=TB),
                    in_=x1t[:, tc * TB:(tc + 1) * TB]
                    .rearrange("(c p) t -> p c t", p=128))
                p.inc(ins, "ld", 16)
            body("sync", _)
            xv = p.cnt("ld")

            def _(e, tc=tc, xv=xv):
                p.wait(e, "ld", xv)
                ins = None
                for c in range(NCH):
                    ins = e.activation(X2[:, c * TB:(c + 1) * TB],
                                       X2[:, c * TB:(c + 1) * TB], AF.Identity,
                                       bias=CST[:, C_BO + c:C_BO + c + 1])
                p.inc(ins, "s")
            body("scalar", _)
            s_x2 = p.cnt("s")
            p.free["X2p"] = ("s", s_x2)

            for h in range(f_hs):
                g_ = h // 4
                def _(e, tc=tc, h=h):
                    p.guard(e, "A1")
                    p.guard(e, "A5q0")
                    p.wait(e, "g", g_after_p2)
                    p.inc(e.dma_start(out=A1[:, 0:4096], in_=kt_hi[h]), "ld", 16)
                    p.inc(e.dma_start(out=A1[:, 4096:8192], in_=kt_lo[h]), "ld", 16)
                    p.inc(e.dma_start(out=A5[:, 0:512],
                                      in_=qt_hi[h][:, tc * 512:(tc + 1) * 512]),
                          "ld", 16)
                    p.inc(e.dma_start(out=A5[:, 512:1024],
                                      in_=qt_lo[h][:, tc * 512:(tc + 1) * 512]),
                          "ld", 16)
                body("sync", _)
                ldv = p.cnt("ld")

                def _(e, tc=tc, h=h):
                    p.guard(e, "A2")
                    p.wait(e, "sc", sc_after_p2)
                    p.wait(e, "g", g_after_p2)
                    p.inc(e.dma_start(
                        out=A2[:, 0:4096],
                        in_=v_hi[h].rearrange("p T w -> p (T w)")), "sc", 16)
                    p.inc(e.dma_start(
                        out=A2[:, 4096:8192],
                        in_=v_lo[h].rearrange("p T w -> p (T w)")), "sc", 16)
                body("scalar", _)
                scv = p.cnt("sc")

                st4 = [(pA, "pA"), (pB, "pB"), (pD, "pD"), (pE, "pE")]
                RACC = SCR[:, 2048:2560]
                OF = SCR[:, 2560:3072]

                def emit_st(e, sq, first_ld=None):
                    bank, bname = st4[sq % 4]
                    if first_ld is not None:
                        p.wait(e, "ld", first_ld)
                    p.guard(e, bname)
                    kh = A1[:, sq * 128:(sq + 1) * 128]
                    kl = A1[:, 4096 + sq * 128:4096 + (sq + 1) * 128]
                    e.matmul(bank[:, :], kh, A5[:, 0:512],
                             start=True, stop=False)
                    e.matmul(bank[:, :], kh, A5[:, 512:1024],
                             start=False, stop=False)
                    ins = e.matmul(bank[:, :], kl, A5[:, 0:512],
                                   start=False, stop=True)
                    p.inc(ins, "t")

                def _(e, h=h, ldv=ldv, scv=scv):
                    p.guard(e, "pC")
                    p.wait(e, "sc", scv)
                    for sq in range(4):
                        emit_st(e, sq, first_ld=(ldv if sq == 0 else None))
                body("tensor", _)
                st_tv = [p.cnt("t") - 3 + i for i in range(4)]

                pv_pending = None
                for sq in range(32):
                    par = sq % 4
                    bank, bname = st4[par]
                    tv = st_tv[sq]

                    def _(e, h=h, sq=sq, par=par, bank=bank, tv=tv):
                        p.wait(e, "t", tv)
                        p.guard(e, f"SCR{par}")
                        p.guard(e, f"A6_{par}")
                        pf = SCR[:, par * 512:par * 512 + 512]
                        e.activation(pf, bank[:, :], AF.Exp,
                                     bias=MCOL[:, h:h + 1], scale=SCALE)
                        ins = e.activation(A6[:, par * 512:par * 512 + 512],
                                           pf, AF.Exp,
                                           bias=MCOL[:, h:h + 1], scale=SCALE) \
                            if False else e.copy(
                                A6[:, par * 512:par * 512 + 512], pf)
                        p.inc(ins, "s")
                    body("scalar", _)
                    p.free[bname] = ("s", p.cnt("s"))
                    sv = p.cnt("s")

                    def _(e, sq=sq, par=par, sv=sv):
                        p.wait(e, "s", sv)
                        p.guard(e, f"A6_{4 + par}")
                        pf = SCR[:, par * 512:par * 512 + 512]
                        if sq == 0:
                            p.guard(e, "SCR4")
                            e.tensor_copy(RACC, pf)
                        else:
                            e.tensor_tensor(RACC, RACC, pf, OP.add)
                        ins = e.tensor_tensor(
                            A6[:, 2048 + par * 512:2048 + par * 512 + 512], pf,
                            A6[:, par * 512:par * 512 + 512], OP.subtract)
                        p.inc(ins, "v")
                    body("vector", _)
                    p.free[f"SCR{par}"] = ("v", p.cnt("v"))
                    vcnt = p.cnt("v")

                    def _(e, sq=sq, par=par, vcnt=vcnt, h=h):
                        p.wait(e, "v", vcnt)
                        vh = A2[:, sq * 128:(sq + 1) * 128]
                        vl = A2[:, 4096 + sq * 128:4096 + (sq + 1) * 128]
                        ph = A6[:, par * 512:par * 512 + 512]
                        pl = A6[:, 2048 + par * 512:2048 + par * 512 + 512]
                        e.matmul(pC[:, :], vh, ph, start=(sq == 0), stop=False)
                        e.matmul(pC[:, :], vh, pl, start=False, stop=False)
                        ins = e.matmul(pC[:, :], vl, ph, start=False,
                                       stop=(sq == 31))
                        p.inc(ins, "t")
                        if sq < 28:
                            emit_st(e, sq + 4)
                    body("tensor", _)
                    p.free[f"A6_{par}"] = ("t", p.cnt("t"))
                    p.free[f"A6_{4 + par}"] = ("t", p.cnt("t"))
                    if sq < 28:
                        st_tv.append(p.cnt("t"))

                p.free["A5q0"] = ("t", p.cnt("t"))
                p.free["A1"] = ("t", p.cnt("t"))
                p.free["A2"] = ("t", p.cnt("t"))

                def _(e):
                    p.guard(e, "pR")
                    ins = e.matmul(pR[0:1, :], CST[:, C_ONES:C_ONES + 1],
                                   RACC, start=True, stop=True)
                    p.inc(ins, "t")
                body("tensor", _)
                p.free["SCR4"] = ("t", p.cnt("t"))
                tv = p.cnt("t")

                def _(e, tv=tv):
                    p.wait(e, "t", tv)
                    ins = e.reciprocal(RSUM[0:1, :], pR[0:1, :])
                    p.inc(ins, "v")
                body("vector", _)
                p.rel("pR", "v")
                vv = p.cnt("v")

                def _(e, vv=vv):
                    p.wait(e, "v", vv)
                    p.guard(e, "pM")
                    ins = e.matmul(pM[:, :], ROW1[0:1, :], RSUM[0:1, :],
                                   start=True, stop=True)
                    p.inc(ins, "t")
                body("tensor", _)
                tv = p.cnt("t")

                def _(e, h=h, tv=tv):
                    p.wait(e, "t", tv)
                    p.guard(e, "A3")
                    e.tensor_copy(MRB[:, 0:512], pM[:, :])
                    of = SCR[:, 2560:3072]
                    e.tensor_tensor(of, pC[:, :], MRB[:, 0:512], OP.mult)
                    hh = h % 4
                    e.tensor_copy(A3[:, hh * 512:(hh + 1) * 512], of)
                    ins = e.tensor_tensor(A3[:, 2048 + hh * 512:
                                             2048 + (hh + 1) * 512], of,
                                          A3[:, hh * 512:(hh + 1) * 512],
                                          OP.subtract)
                    p.inc(ins, "v")
                body("vector", _)
                p.rel("pM", "v")
                p.rel("pC", "v")
                p.free["SCR3"] = ("v", p.cnt("v"))
                v_og = p.cnt("v")

                if h % 4 == 3:
                    pj_banks = [(pD, "pD"), (pE, "pE")]
                    for c in range(NCH):
                        par = c % 2

                        def _(e, tc=tc, h=h, c=c, par=par):
                            p.guard(e, f"A4w{par}")
                            p.wait(e, "sc", sc_after_p0)
                            ins = e.dma_start(
                                out=A4[:, par * 512:par * 512 + 512]
                                .rearrange("p (j w) -> p j w", w=128),
                                in_=wob[c, (h - 3):(h + 1)]
                                .rearrange("j p w -> p j w"))
                            p.inc(ins, "ld", 16)
                        body("sync", _)
                        ldv = p.cnt("ld")
                        bank, bname = pj_banks[par]

                        def _(e, c=c, par=par, ldv=ldv, bank=bank, bname=bname,
                              v_og=v_og):
                            p.wait(e, "ld", ldv)
                            if c == 0:
                                p.wait(e, "v", v_og)
                            p.guard(e, bname)
                            for j in range(4):
                                wt = A4[:, par * 512 + j * 128:
                                        par * 512 + (j + 1) * 128]
                                e.matmul(bank[:, :], wt,
                                         A3[:, j * 512:(j + 1) * 512],
                                         start=(j == 0), stop=False)
                                ins = e.matmul(bank[:, :], wt,
                                               A3[:, 2048 + j * 512:
                                                  2048 + (j + 1) * 512],
                                               start=False, stop=(j == 3))
                            p.inc(ins, "t")
                        body("tensor", _)
                        p.free[f"A4w{par}"] = ("t", p.cnt("t"))
                        tv = p.cnt("t")

                        def _(e, c=c, tv=tv, bank=bank, s_x2=s_x2):
                            p.wait(e, "t", tv)
                            if c == 0:
                                p.wait(e, "s", s_x2)
                            ins = e.tensor_tensor(X2[:, c * TB:(c + 1) * TB],
                                                  X2[:, c * TB:(c + 1) * TB],
                                                  bank[:, :], OP.add)
                            p.inc(ins, "v")
                        body("vector", _)
                        p.free[bname] = ("v", p.cnt("v"))
                    p.free["A3"] = ("t", p.cnt("t"))

            if stop == 'flash':
                v_done = p.cnt("v")
            else:
                v_done = emit_ffn(("v", p.cnt("v")), g_after_p0)

            def _(e, tc=tc, v_done=v_done):
                p.wait(e, "v", v_done)
                ins = e.dma_start(
                    out=out_t[:, tc * TB:(tc + 1) * TB]
                    .rearrange("(c p) t -> p c t", p=128),
                    in_=X2[:, :].rearrange("p (c t) -> p c t", t=TB))
                p.inc(ins, "g", 16)
            body("gpsimd", _)
            p.rel("X2", "g")

        g_final = p.cnt("g")

        def _(e, g_final=g_final):
            p.wait(e, "g", g_final)
        body("gpsimd", _)

    return nc


_NC_CACHE = {}


def kernel(L0_attn_x, L0_attn_wq_codes, L0_attn_wq_scales, L0_attn_bq,
           L0_attn_wk_codes, L0_attn_wk_scales, L0_attn_bk,
           L0_attn_wv_codes, L0_attn_wv_scales, L0_attn_bv,
           L0_attn_wo_codes, L0_attn_wo_scales, L0_attn_bo,
           L0_ffn_gate_w, L0_ffn_up_w, L0_ffn_down_w,
           L0_ffn_gate_scale, L0_ffn_up_scale, L0_ffn_down_scale):
    x = np.asarray(L0_attn_x, dtype=np.float32)

    def col2(b):
        return np.ascontiguousarray(np.asarray(b, np.float32).reshape(NCH, 128).T)

    def col2w(b):
        return np.ascontiguousarray(np.asarray(b, np.float32).reshape(NO, 128).T)

    shared = {
        "wq_codes": np.ascontiguousarray(L0_attn_wq_codes, np.float32),
        "wk_codes": np.ascontiguousarray(L0_attn_wk_codes, np.float32),
        "wv_codes": np.ascontiguousarray(L0_attn_wv_codes, np.float32),
        "wo_codes": np.ascontiguousarray(L0_attn_wo_codes, np.float32),
        "wq_sx": np.ascontiguousarray(np.repeat(np.asarray(L0_attn_wq_scales, np.float32), 32, axis=0)),
        "wk_sx": np.ascontiguousarray(np.repeat(np.asarray(L0_attn_wk_scales, np.float32), 32, axis=0)),
        "wv_sx": np.ascontiguousarray(np.repeat(np.asarray(L0_attn_wv_scales, np.float32), 32, axis=0)),
        "wo_sx": np.ascontiguousarray(np.repeat(np.asarray(L0_attn_wo_scales, np.float32), 32, axis=0)),
        "bq2": col2(L0_attn_bq), "bk2": col2(L0_attn_bk), "bo2": col2(L0_attn_bo),
        "bv_row": np.ascontiguousarray(np.asarray(L0_attn_bv, np.float32).reshape(1, D)),
        "gate_w": np.ascontiguousarray(L0_ffn_gate_w, np.float32),
        "up_w": np.ascontiguousarray(L0_ffn_up_w, np.float32),
        "down_w": np.ascontiguousarray(L0_ffn_down_w, np.float32),
        "gs2": col2w(L0_ffn_gate_scale), "us2": col2w(L0_ffn_up_scale),
        "ds2": col2(L0_ffn_down_scale),
    }

    in_maps = []
    for core in range(8):
        b, half = core // 2, core % 2
        own = x[b, half * TOWN:(half + 1) * TOWN, :]
        other = x[b, (1 - half) * TOWN:(2 - half) * TOWN, :]
        x0t = np.ascontiguousarray(np.concatenate([own, other], axis=0).T)
        m = dict(shared)
        m["x0t"] = x0t
        in_maps.append(m)

    if "nc" not in _NC_CACHE:
        _NC_CACHE["nc"] = build()
    nc = _NC_CACHE["nc"]
    res = run_bass_kernel_spmd(nc, in_maps, core_ids=list(range(8)))

    out = np.empty((4, S, D), dtype=np.float32)
    for core in range(8):
        b, half = core // 2, core % 2
        out[b, half * TOWN:(half + 1) * TOWN, :] = res.results[core]["out_t"].T
    return out


if __name__ == "__main__":
    import time
    t0 = time.time()
    nc = build()
    print(f"build {time.time() - t0:.1f}s")
